# version 15
# baseline (speedup 1.0000x reference)
import numpy as np
import ml_dtypes
from contextlib import ExitStack

import concourse.bass as bass
import concourse.mybir as mybir
from concourse.bass_utils import run_bass_kernel_spmd

F32 = mybir.dt.float32
BF16 = mybir.dt.bfloat16
AF = mybir.ActivationFunctionType
ALU = mybir.AluOpType
AX = mybir.AxisListType

D = 1024
T = 2048
I = 1408
NE = 10
NI = 11
NTC = 16
RING = 27
EPS = 1e-5

SCR_BF = 55296
ACT_OFF = 27648
SG_F = 25088
FIL_F = (ACT_OFF + 8 * 2048) // 2
ZT2_F = 3072
WRT_BF = 0
WIT_BF = 8192
R_F = 8192
I_F = 12288
Z2_F = 16384
LA_F = 20480
A_F = 22528
ZT_F = 24576


def _host_layout(inputs):
    g = {k: np.ascontiguousarray(np.asarray(v, dtype=np.float32))
         for k, v in inputs.items()}
    x = g["L0_attn_x"]

    def lay_up(w):
        return w.reshape(8, 128, 11, 128).transpose(2, 1, 0, 3)

    wg_l, wu_l, wd_l = [], [], []
    for e in range(8):
        wg_l.append(lay_up(g["L0_ffn_wg"][e]))
        wu_l.append(lay_up(g["L0_ffn_wu"][e]))
        wd_l.append(g["L0_ffn_wd"][e].reshape(11, 128, 1024))
    for h in range(2):
        wg_l.append(lay_up(g["L0_ffn_swg"][:, h * I:(h + 1) * I]))
        wu_l.append(lay_up(g["L0_ffn_swu"][:, h * I:(h + 1) * I]))
        wd_l.append(g["L0_ffn_swd"][h * I:(h + 1) * I, :].reshape(11, 128, 1024))
    wgs = np.ascontiguousarray(np.stack(wg_l))
    wus = np.ascontiguousarray(np.stack(wu_l))
    wds = np.ascontiguousarray(np.stack(wd_l))

    def lay_sq(w):
        return np.ascontiguousarray(w.reshape(8, 128, 1024).transpose(1, 0, 2))

    wrt = lay_sq(np.ascontiguousarray(g["L0_attn_W_r"].T))
    wit = lay_sq(np.ascontiguousarray(g["L0_attn_W_i"].T))
    gwt = np.ascontiguousarray(
        g["L0_ffn_gate_w"].T.reshape(8, 128, 8).transpose(1, 0, 2))
    lam = np.ascontiguousarray(g["L0_attn_lambd"].reshape(8, 128).T)
    idb = np.eye(128, dtype=ml_dtypes.bfloat16)
    idf = np.eye(128, dtype=np.float32)
    cst = np.tile(np.array([[EPS, 0.0, 0.0, 0.0]], dtype=np.float32), (128, 1))

    xs = [np.ascontiguousarray(x[b].reshape(16, 128, 1024).transpose(1, 0, 2))
          for b in range(8)]
    shared = dict(wgs=wgs, wus=wus, wds=wds, wrt=wrt, wit=wit, gwt=gwt,
                  lam=lam, idb=idb, idf=idf, cst=cst)
    return xs, shared


def build_nc(stages=("ffn1", "ssm", "ffn3")):
    nc = bass.Bass("TRN2", target_bir_lowering=False)

    x_d = nc.dram_tensor("x", [128, 16, 1024], F32, kind="ExternalInput").ap()
    wgs_d = nc.dram_tensor("wgs", [NE, NI, 128, 8, 128], F32, kind="ExternalInput").ap()
    wus_d = nc.dram_tensor("wus", [NE, NI, 128, 8, 128], F32, kind="ExternalInput").ap()
    wds_d = nc.dram_tensor("wds", [NE, NI, 128, 1024], F32, kind="ExternalInput").ap()
    wrt_d = nc.dram_tensor("wrt", [128, 8, 1024], F32, kind="ExternalInput").ap()
    wit_d = nc.dram_tensor("wit", [128, 8, 1024], F32, kind="ExternalInput").ap()
    gwt_d = nc.dram_tensor("gwt", [128, 8, 8], F32, kind="ExternalInput").ap()
    lam_d = nc.dram_tensor("lam", [128, 8], F32, kind="ExternalInput").ap()
    idb_d = nc.dram_tensor("idb", [128, 128], BF16, kind="ExternalInput").ap()
    idf_d = nc.dram_tensor("idf", [128, 128], F32, kind="ExternalInput").ap()
    cst_d = nc.dram_tensor("cst", [128, 4], F32, kind="ExternalInput").ap()
    out_d = nc.dram_tensor("out", [128, 16, 1024], F32, kind="ExternalOutput").ap()
    dbg = any(str(s).startswith("dump") or str(s).startswith("nexp") for s in stages)
    if dbg:
        zdump_d = nc.dram_tensor("zdump", [128, 8, 2048], BF16, kind="ExternalOutput").ap()
        wfdump_d = nc.dram_tensor("wfdump", [128, 16, 8], F32, kind="ExternalOutput").ap()
    if "dump_ssm" in stages:
        sdump_d = nc.dram_tensor("sdump", [128, 5, 8, 512], F32, kind="ExternalOutput").ap()
        ldump_d = nc.dram_tensor("ldump", [128, 16, 2], F32, kind="ExternalOutput").ap()

    with ExitStack() as stack:
        E = stack.enter_context
        x_sb = E(nc.sbuf_tensor("x_sb", [128, 16, 1024], F32))
        z_sb = E(nc.sbuf_tensor("z_sb", [128, 8, 2048], BF16))
        scr = E(nc.sbuf_tensor("scr", [128, SCR_BF], BF16))
        gwt_sb = E(nc.sbuf_tensor("gwt_sb", [128, 8, 8], BF16))
        lam_sb = E(nc.sbuf_tensor("lam_sb", [128, 8], F32))
        l8_sb = E(nc.sbuf_tensor("l8_sb", [128, 8], F32))
        cst_sb = E(nc.sbuf_tensor("cst_sb", [128, 4], F32))
        idb_sb = E(nc.sbuf_tensor("idb_sb", [128, 128], BF16))
        idf_sb = E(nc.sbuf_tensor("idf_sb", [128, 128], F32))
        wfin_sb = E(nc.sbuf_tensor("wfin_sb", [128, 16, 8], F32))
        wfin_fl = wfin_sb.reshape([128, 128])
        ltm_sb = E(nc.sbuf_tensor("ltm_sb", [128, 16, 8], F32))
        lgt_sb = E(nc.sbuf_tensor("lgt_sb", [128, 16, 8], F32))
        stat_sb = E(nc.sbuf_tensor("stat_sb", [128, 48], F32))
        ln2_sb = E(nc.sbuf_tensor("ln2_sb", [128, 16, 2], F32))
        ln2_fl = ln2_sb.reshape([128, 32])
        carry_sb = E(nc.sbuf_tensor("carry_sb", [128, 8], F32))
        gmax_sb = E(nc.sbuf_tensor("gmax_sb", [128, 16], F32))
        gm2_sb = E(nc.sbuf_tensor("gm2_sb", [128, 16], F32))
        gden_sb = E(nc.sbuf_tensor("gden_sb", [128, 16], F32))

        PT = {
            "pA": E(nc.psum_tensor("pA", [128, 1024], F32)),
            "pB": E(nc.psum_tensor("pB", [128, 1024], F32)),
            "yA": E(nc.psum_tensor("yA", [128, 1024], F32)),
            "yB": E(nc.psum_tensor("yB", [128, 1024], F32)),
        }

        SEM = {
            "t": E(nc.semaphore("sem_t")),
            "v": E(nc.semaphore("sem_v")),
            "s": E(nc.semaphore("sem_s")),
            "g": E(nc.semaphore("sem_g")),
            "q": E(nc.semaphore("sem_q")),
            "f": E(nc.semaphore("sem_f")),
            "G": E(nc.semaphore("sem_G")),
            "Q": E(nc.semaphore("sem_Q")),
        }

        scrf = scr.bitcast(F32)

        ops = {k: [] for k in "tvsgq"}
        cnt = {"t": 0, "v": 0, "s": 0, "f": 0, "G": 0, "Q": 0}
        dma = {"g": 0, "q": 0}
        known = {k: {} for k in "tvsgq"}

        def wait(eng, key, val):
            if val is None or val <= 0:
                return
            if val <= known[eng].get(key, 0):
                return
            known[eng][key] = val
            ops[eng].append(lambda E_, s=SEM[key], v=val: E_.wait_ge(s, v))

        def emit(eng, fn, incs=()):
            norm = [(k, -1) if isinstance(k, str) else k for k in incs]
            attach = [(k, i) for k, i in norm if i is not None]
            post = [k for k, i in norm if i is None]
            idxs = [i for _, i in attach]
            assert len(set(idxs)) == len(idxs)
            own = cnt[eng] + sum(1 for k, i in attach if k == eng)
            def run(E_, fn=fn, attach=tuple(attach), post=tuple(post), own=own):
                insts = fn(E_)
                for k, idx in attach:
                    insts[idx].then_inc(SEM[k], 1)
                for k in post:
                    E_.wait_ge(SEM[eng], own).then_inc(SEM[k], 1)
            ops[eng].append(run)
            for k, _ in norm:
                cnt[k] += 1

        def dma_emit(eng, fn):
            def run(E_, fn=fn):
                fn(E_).then_inc(SEM[eng], 16)
            ops[eng].append(run)
            dma[eng] += 16
            return dma[eng]

        def dma_flush(eng):
            pub = "G" if eng == "g" else "Q"
            v = dma[eng]
            def run(E_, v=v):
                E_.wait_ge(SEM[eng], v)
                E_.nop().then_inc(SEM[pub], 1)
            ops[eng].append(run)
            cnt[pub] += 1
            return cnt[pub]

        pg = {k: None for k in PT}
        ztg_map = {}
        sqm_map = {}
        sgg = [None, None]

        def map_wait(eng, mp, key):
            g = mp.get(key)
            if g is not None:
                wait(eng, g[0], g[1])

        def res_wait(eng, guards, idxs):
            for n in idxs:
                g = guards[n]
                if g is not None:
                    wait(eng, g[0], g[1])

        def psum_wait(eng, names):
            for n in names:
                if pg[n] is not None:
                    wait(eng, pg[n][0], pg[n][1])

        def psum_set(names, key, val):
            for n in names:
                pg[n] = (key, val)

        def ring_slot(k):
            s0 = (k % RING) * 1024
            return scr[:, s0:s0 + 1024]

        def act_ap(i):
            o = ACT_OFF + i * 2048
            return scr[:, o:o + 2048]

        def sg_ap(p):
            o = SG_F + p * 1024
            return scrf[:, o:o + 1024]

        def zt_ap(p):
            o = ZT_F + p * 1024
            return scrf[:, o:o + 1024]

        def zt_bf(p):
            o = 2 * (ZT_F + p * 1024)
            return scr[:, o:o + 1024]

        def fil_ap():
            return scrf[:, FIL_F:FIL_F + 512]

        x_ready = []
        for c in range(4):
            dma_emit("q", lambda E_, c=c: E_.dma_start(
                x_sb[:, 4 * c:4 * c + 4, :], x_d[:, 4 * c:4 * c + 4, :]))
            x_ready.append(dma_flush("q"))
        dma_emit("q", lambda E_: E_.dma_start(cst_sb[:, :], cst_d))
        dma_emit("q", lambda E_: E_.dma_start(lam_sb[:, :], lam_d))
        dma_emit("q", lambda E_: E_.dma_start(idb_sb[:, :], idb_d))
        dma_emit("q", lambda E_: E_.dma_start(idf_sb[:, :], idf_d))
        cst_ready = lam_ready = idb_ready = idf_ready = dma_flush("q")
        dma_emit("g", lambda E_: E_.dma_start(gwt_sb[:, :, :], gwt_d))
        gwt_ready = dma_flush("g")

        wait("s", "Q", lam_ready)
        emit("s", lambda E_: [E_.activation(l8_sb[:, :], lam_sb[:, :], AF.Sigmoid)],
             incs=["s"])
        wait("s", "s", cnt["s"])
        emit("s", lambda E_: [E_.activation(l8_sb[:, :], l8_sb[:, :], AF.Ln)],
             incs=["s"])
        l8_sm = cnt["s"]
        wait("v", "s", l8_sm)
        emit("v", lambda E_: [E_.tensor_scalar(l8_sb[:, :], l8_sb[:, :], 8.0,
                                               None, ALU.mult)], incs=["v"])
        l8_vm = cnt["v"]

        class LNPipe:
            def __init__(self, save_stats, zt0, zt1, psA, psB):
                self.save = save_stats
                self.zt = (zt0, zt1)
                self.ps = (psA, psB)
                self.k = 0
                self.z_evac = []
                self.sq_sm = [0] * 6
                self.red_vm = [0] * 6
                self.m_sm = [0] * 6
                self.nvs_vm = [0] * 6
                self.sd_sm = [0] * 6
                self.rstd_vm = [0] * 6

            def zt_f(self, p):
                o = self.zt[p]
                return scrf[:, o:o + 1024]

            def zt_b(self, p):
                o = 2 * self.zt[p]
                return scr[:, o:o + 1024]

            def step(self, xdep=None):
                k = self.k
                self.k += 1
                if k < NTC:
                    tc = k
                    par = (k % 6) * 8
                    sl = k % 2
                    off = self.zt[sl]
                    xc = x_sb[:, tc, :]
                    wait("s", "Q", x_ready[tc // 4])
                    wait("s", "Q", cst_ready)
                    if xdep is not None:
                        wait("s", xdep[0], xdep[1])
                    map_wait("s", ztg_map, off)
                    dsc = self.zt_f(sl)
                    emit("s", lambda E_, xc=xc, dsc=dsc, par=par: [
                        E_.activation(dsc, xc, AF.Square,
                                      accum_out=stat_sb[:, par + 1:par + 2])],
                        incs=["s"])
                    sqm_map[off] = ("s", cnt["s"])
                    self.sq_sm[k % 6] = cnt["s"]
                    wait("v", "Q", x_ready[tc // 4])
                    if xdep is not None:
                        wait("v", xdep[0], xdep[1])
                    emit("v", lambda E_, xc=xc, par=par: [
                        E_.reduce_sum(stat_sb[:, par + 0:par + 1], xc, AX.X)],
                        incs=["v"])
                    self.red_vm[k % 6] = cnt["v"]
                if 1 <= k <= NTC:
                    p1 = ((k - 1) % 6) * 8
                    wait("s", "v", self.red_vm[(k - 1) % 6])
                    wait("s", "s", self.sq_sm[(k - 1) % 6])
                    emit("s", lambda E_, p1=p1: [
                        E_.mul(stat_sb[:, p1 + 2:p1 + 3],
                               stat_sb[:, p1 + 0:p1 + 1], 1.0 / 1024.0)])
                    emit("s", lambda E_, p1=p1: [
                        E_.mul(stat_sb[:, p1 + 3:p1 + 4],
                               stat_sb[:, p1 + 1:p1 + 2], 1.0 / 1024.0)],
                        incs=["s"])
                    self.m_sm[(k - 1) % 6] = cnt["s"]
                if 2 <= k <= NTC + 1:
                    p2 = ((k - 2) % 6) * 8
                    wait("v", "s", self.m_sm[(k - 2) % 6])
                    emit("v", lambda E_, p2=p2: [
                        E_.scalar_tensor_tensor(
                            stat_sb[:, p2 + 4:p2 + 5],
                            stat_sb[:, p2 + 2:p2 + 3],
                            stat_sb[:, p2 + 2:p2 + 3],
                            stat_sb[:, p2 + 3:p2 + 4],
                            ALU.mult, ALU.subtract)], incs=["v"])
                    self.nvs_vm[(k - 2) % 6] = cnt["v"]
                if 3 <= k <= NTC + 2:
                    p3 = ((k - 3) % 6) * 8
                    wait("s", "v", self.nvs_vm[(k - 3) % 6])
                    emit("s", lambda E_, p3=p3: [
                        E_.activation(stat_sb[:, p3 + 5:p3 + 6],
                                      stat_sb[:, p3 + 4:p3 + 5], AF.Sqrt,
                                      scale=-1.0, bias=cst_sb[:, 0:1])],
                        incs=["s"])
                    self.sd_sm[(k - 3) % 6] = cnt["s"]
                if 4 <= k <= NTC + 3:
                    p4 = ((k - 4) % 6) * 8
                    wait("v", "s", self.sd_sm[(k - 4) % 6])
                    emit("v", lambda E_, p4=p4: [
                        E_.reciprocal(stat_sb[:, p4 + 6:p4 + 7],
                                      stat_sb[:, p4 + 5:p4 + 6])], incs=["v"])
                    self.rstd_vm[(k - 4) % 6] = cnt["v"]
                if 5 <= k:
                    tc = k - 5
                    p5 = (tc % 6) * 8
                    sl2 = tc % 2
                    off2 = self.zt[sl2]
                    wait("v", "v", self.rstd_vm[tc % 6])
                    if self.save:
                        emit("v", lambda E_, p5=p5, tc=tc: [
                            E_.tensor_copy(ln2_fl[:, tc * 2:tc * 2 + 1],
                                           stat_sb[:, p5 + 2:p5 + 3])])
                        emit("v", lambda E_, p5=p5, tc=tc: [
                            E_.tensor_copy(ln2_fl[:, tc * 2 + 1:tc * 2 + 2],
                                           stat_sb[:, p5 + 6:p5 + 7])])
                    ztm = self.zt_b(sl2)
                    map_wait("v", sqm_map, off2)
                    map_wait("v", ztg_map, off2)
                    emit("v", lambda E_, tc=tc, p5=p5, ztm=ztm: [
                        E_.tensor_scalar(ztm, x_sb[:, tc, :],
                                         stat_sb[:, p5 + 2:p5 + 3],
                                         stat_sb[:, p5 + 6:p5 + 7],
                                         ALU.subtract, ALU.mult)], incs=["v"])
                    ztm_vm = cnt["v"]
                    pname = self.ps[sl2]
                    psb = PT[pname].bitcast(BF16)
                    wait("t", "v", ztm_vm)
                    wait("t", "Q", idb_ready)
                    psum_wait("t", [pname])
                    def tr8(E_, ztm=ztm, psb=psb):
                        insts = []
                        for dc in range(8):
                            insts.append(E_.transpose(
                                psb[:, dc * 128:dc * 128 + 128],
                                ztm[:, dc * 128:dc * 128 + 128], idb_sb[:, :]))
                        return insts
                    emit("t", tr8, incs=["t"])
                    ztg_map[off2] = ("t", cnt["t"])
                    wait("s", "t", cnt["t"])
                    emit("s", lambda E_, psb=psb, tc=tc: [
                        E_.copy(
                            z_sb[:, 0:8, tc * 128:tc * 128 + 128],
                            psb[:, 0:1024].rearrange("p (a b) -> p a b", b=128))],
                        incs=["s"])
                    psum_set([pname], "s", cnt["s"])
                    self.z_evac.append(cnt["s"])

            def finish(self, xdep=None):
                while self.k < NTC + 5:
                    self.step(xdep=xdep)
                return self.z_evac

        def emit_gates(z_evac):
            wait("t", "G", gwt_ready)
            psum_wait("t", ["pA"])
            for tc in range(NTC):
                wait("t", "s", z_evac[tc])
                def gmm(E_, tc=tc):
                    insts = []
                    for kc in range(8):
                        E_.ldweights(z_sb[:, kc, tc * 128:tc * 128 + 128])
                        insts.append(E_.matmul(
                            PT["pA"][:, tc * 8:tc * 8 + 8],
                            z_sb[:, kc, tc * 128:tc * 128 + 128],
                            gwt_sb[:, kc, :],
                            start=(kc == 0), stop=(kc == 7)))
                    return insts
                emit("t", gmm, incs=["t"])
            wait("v", "t", cnt["t"])
            emit("v", lambda E_: [
                E_.tensor_copy(ltm_sb[:, :, :],
                               PT["pA"][:, 0:128].rearrange(
                                   "p (a b) -> p a b", b=8))], incs=["v"])
            psum_set(["pA"], "v", cnt["v"])
            return cnt["v"]

        def emit_gate_math(ltm_vm):
            l3 = ltm_sb[:, :, :]
            p3 = lgt_sb[:, :, :]
            t3 = wfin_sb[:, :, :]
            m1b = gmax_sb[:, :].unsqueeze(2).to_broadcast([128, 16, 8])
            m2b = gm2_sb[:, :].unsqueeze(2).to_broadcast([128, 16, 8])
            rdb = gden_sb[:, :].unsqueeze(2).to_broadcast([128, 16, 8])
            wait("v", "v", ltm_vm)
            emit("v", lambda E_: [E_.tensor_reduce(gmax_sb[:, :], l3, AX.X, ALU.max)],
                 incs=["v"])
            wait("v", "v", cnt["v"])
            emit("v", lambda E_: [E_.tensor_tensor(p3, l3, m1b, ALU.subtract)],
                 incs=["v"])
            wait("s", "v", cnt["v"])
            emit("s", lambda E_: [E_.activation(p3, p3, AF.Exp)], incs=["s"])
            emit("v", lambda E_: [E_.tensor_tensor(t3, l3, m1b, ALU.is_ge)],
                 incs=["v"])
            wait("v", "v", cnt["v"])
            emit("v", lambda E_: [E_.scalar_tensor_tensor(t3, t3, -1e30, l3,
                                                          ALU.mult, ALU.add)],
                 incs=["v"])
            wait("v", "v", cnt["v"])
            emit("v", lambda E_: [E_.tensor_reduce(gm2_sb[:, :], t3, AX.X, ALU.max)],
                 incs=["v"])
            wait("v", "v", cnt["v"])
            emit("v", lambda E_: [E_.tensor_tensor(t3, l3, m2b, ALU.is_ge)],
                 incs=["v"])
            wait("v", "v", cnt["v"])
            wait("v", "s", cnt["s"])
            emit("v", lambda E_: [E_.tensor_tensor(p3, p3, t3, ALU.mult)],
                 incs=["v"])
            wait("v", "v", cnt["v"])
            emit("v", lambda E_: [E_.tensor_reduce(gden_sb[:, :], p3, AX.X, ALU.add)],
                 incs=["v"])
            wait("v", "v", cnt["v"])
            emit("v", lambda E_: [E_.reciprocal(gden_sb[:, :], gden_sb[:, :])],
                 incs=["v"])
            wait("v", "v", cnt["v"])
            emit("v", lambda E_: [E_.tensor_tensor(p3, p3, rdb, ALU.mult)],
                 incs=["v"])
            wait("v", "v", cnt["v"])
            emit("v", lambda E_: [E_.tensor_scalar(wfin_sb[:, :, :], p3, 0.5,
                                                   None, ALU.mult)], incs=["v"])
            return cnt["v"]

        slice_idx = [0]
        unit_hist = []

        def ring_dma(src_ap, flush=True):
            k = slice_idx[0]
            wait("g", "f", k - RING + 1)
            dst = ring_slot(k)
            dma_emit("g", lambda E_, dst=dst, src=src_ap: E_.dma_start(dst, src))
            slice_idx[0] += 1
            v = dma_flush("g") if flush else None
            return k, v

        ymark = [0] * NTC

        def emit_ffn(z_evac, wfin_vm, nexp=NE, e0=0, ln_tail=None):
            z_all = z_evac[-1]
            for e in range(e0, nexp):
                wg_slots, wu_slots, wd_slots = [], [], []
                wg_done, wu_done, wd_done = [], [], []
                for i in range(NI):
                    k, _ = ring_dma(wgs_d[e, i], flush=False)
                    wg_slots.append(k)
                    k, v = ring_dma(wus_d[e, i], flush=True)
                    wu_slots.append(k); wu_done.append(v)
                for i in range(NI):
                    k, v = ring_dma(wds_d[e, i], flush=(i == NI - 1))
                    wd_slots.append(k)
                    if v is not None:
                        wd_done.append(v)

                for i in range(NI):
                    for tp in range(2):
                        par = (i * 2 + tp) % 2
                        hg_n = "pA" if par == 0 else "yA"
                        hu_n = "pB" if par == 0 else "yB"
                        hg, hu = PT[hg_n], PT[hu_n]
                        wait("t", "G", wu_done[i])
                        wait("t", "s", z_evac[7] if tp == 0 else z_evac[15])
                        psum_wait("t", [hg_n, hu_n])
                        wg_ap = ring_slot(wg_slots[i]).rearrange(
                            "p (a b) -> p a b", b=128)
                        wu_ap = ring_slot(wu_slots[i]).rearrange(
                            "p (a b) -> p a b", b=128)
                        t0 = tp * 1024
                        incs = (["t"] if tp == 0
                                else [("t", -1), ("f", 15), ("f", None)])
                        def up(E_, wg_ap=wg_ap, wu_ap=wu_ap, hg=hg, hu=hu, t0=t0):
                            insts = []
                            for d in range(8):
                                E_.ldweights(wg_ap[:, d, :])
                                insts.append(E_.matmul(
                                    hg[:, 0:512], wg_ap[:, d, :],
                                    z_sb[:, d, t0:t0 + 512],
                                    start=(d == 0), stop=(d == 7)))
                                insts.append(E_.matmul(
                                    hg[:, 512:1024], wg_ap[:, d, :],
                                    z_sb[:, d, t0 + 512:t0 + 1024],
                                    start=(d == 0), stop=(d == 7)))
                            for d in range(8):
                                E_.ldweights(wu_ap[:, d, :])
                                insts.append(E_.matmul(
                                    hu[:, 0:512], wu_ap[:, d, :],
                                    z_sb[:, d, t0:t0 + 512],
                                    start=(d == 0), stop=(d == 7)))
                                insts.append(E_.matmul(
                                    hu[:, 512:1024], wu_ap[:, d, :],
                                    z_sb[:, d, t0 + 512:t0 + 1024],
                                    start=(d == 0), stop=(d == 7)))
                            return insts
                        emit("t", up, incs=incs)
                        tmark = cnt["t"]
                        wait("s", "t", tmark)
                        res_wait("s", sgg, [par])
                        emit("s", lambda E_, hg=hg, par=par: [
                            E_.activation(sg_ap(par), hg[:, :], AF.Silu)],
                            incs=["s"])
                        wait("v", "s", cnt["s"])
                        dst = act_ap(i)[:, tp * 1024:tp * 1024 + 1024]
                        emit("v", lambda E_, hu=hu, par=par, dst=dst: [
                            E_.tensor_tensor(dst, hu[:, :], sg_ap(par), ALU.mult)],
                            incs=["v"])
                        vm = cnt["v"]
                        psum_set([hg_n, hu_n], "v", vm)
                        sgg[par] = ("v", vm)
                acts_vm = cnt["v"]

                for tc in range(NTC):
                    y_n = "pA" if tc % 2 == 0 else "yA"
                    yps = PT[y_n]
                    wait("t", "G", wd_done[-1])
                    wait("t", "v", acts_vm)
                    psum_wait("t", [y_n])
                    last_group = (tc == NTC - 1)
                    if last_group:
                        incs = ([("t", -1)]
                                + [("f", 2 * i + 2) for i in range(NI - 1)]
                                + [("f", None)])
                    else:
                        incs = ["t"]
                    def down(E_, tc=tc, yps=yps, wd_slots=tuple(wd_slots)):
                        insts = []
                        for i in range(NI):
                            a_ap = act_ap(i)[:, tc * 128:tc * 128 + 128]
                            w_ap = ring_slot(wd_slots[i])
                            E_.ldweights(a_ap)
                            insts.append(E_.matmul(
                                yps[:, 0:512], a_ap, w_ap[:, 0:512],
                                start=(i == 0), stop=(i == NI - 1)))
                            insts.append(E_.matmul(
                                yps[:, 512:1024], a_ap, w_ap[:, 512:1024],
                                start=(i == 0), stop=(i == NI - 1)))
                        return insts
                    emit("t", down, incs=incs)
                    wait("v", "t", cnt["t"])
                    wait("v", "v", ymark[tc])
                    if e < 8:
                        wait("v", "v", wfin_vm)
                        sc_ap = wfin_fl[:, tc * 8 + e:tc * 8 + e + 1]
                        emit("v", lambda E_, yps=yps, tc=tc, sc_ap=sc_ap: [
                            E_.scalar_tensor_tensor(
                                x_sb[:, tc, :], yps[:, :], sc_ap,
                                x_sb[:, tc, :], ALU.mult, ALU.add)], incs=["v"])
                    else:
                        emit("v", lambda E_, yps=yps, tc=tc: [
                            E_.scalar_tensor_tensor(
                                x_sb[:, tc, :], yps[:, :], 0.5,
                                x_sb[:, tc, :], ALU.mult, ALU.add)], incs=["v"])
                    ymark[tc] = cnt["v"]
                    psum_set([y_n], "v", cnt["v"])
                    if ln_tail is not None and e == nexp - 1:
                        ln_tail.step(xdep=("v", cnt["v"]))
            return cnt["v"]

        def emit_ssm(z_evac, ring_free_dep, w_deps=(), ln_tail=None, xvm=0):
            wait("g", "f", ring_free_dep)
            for key_, val_ in w_deps:
                wait("g", key_, val_)
            wrt_sb = scr[:, WRT_BF:WRT_BF + 8192].rearrange("p (a b) -> p a b", b=1024)
            wit_sb = scr[:, WIT_BF:WIT_BF + 8192].rearrange("p (a b) -> p a b", b=1024)
            dma_emit("g", lambda E_: E_.dma_start(wrt_sb, wrt_d))
            dma_emit("g", lambda E_: E_.dma_start(wit_sb, wit_d))
            wit_done = dma_flush("g")

            r_ap = scrf[:, R_F:R_F + 4096].rearrange("p (a b) -> p a b", b=512)
            i_ap = scrf[:, I_F:I_F + 4096].rearrange("p (a b) -> p a b", b=512)
            z2_ap = scrf[:, Z2_F:Z2_F + 4096].rearrange("p (a b) -> p a b", b=512)
            a8_ap = scrf[:, LA_F:LA_F + 4096].rearrange("p (a b) -> p a b", b=512)

            RIP = [("yA", 0), ("yA", 512), ("yB", 0), ("yB", 512)]
            rip_guard = [pg["yA"], pg["yA"], pg["yB"], pg["yB"]]

            z_all = z_evac[-1]
            bi_vm = [0, 0]
            chain_sm = [0, 0]
            scan_vm = [0, 0]
            bt_vm = [0, 0]

            def emit_A1(tt):
                for q in range(4):
                    tc = tt * 4 + q
                    par = q % 2
                    zt = zt_ap(par)
                    mcol = ln2_fl[:, tc * 2:tc * 2 + 1]
                    rstd = ln2_fl[:, tc * 2 + 1:tc * 2 + 2]
                    zoff = ZT_F + par * 1024
                    map_wait("v", ztg_map, zoff)
                    map_wait("v", sqm_map, zoff)
                    wait("v", "v", xvm)
                    emit("v", lambda E_, tc=tc, zt=zt, mcol=mcol, rstd=rstd: [
                        E_.tensor_scalar(zt, x_sb[:, tc, :], mcol, rstd,
                                         ALU.subtract, ALU.mult)], incs=["v"])
                    pname = "pA" if par == 0 else "pB"
                    ps = PT[pname]
                    wait("t", "v", cnt["v"])
                    wait("t", "Q", idf_ready)
                    psum_wait("t", [pname])
                    def trz(E_, zt=zt, ps=ps):
                        insts = []
                        for dc in range(8):
                            insts.append(E_.transpose(
                                ps[:, dc * 128:dc * 128 + 128],
                                zt[:, dc * 128:dc * 128 + 128], idf_sb[:, :]))
                        return insts
                    emit("t", trz, incs=["t"])
                    ztg_map[zoff] = ("t", cnt["t"])
                    wait("s", "t", cnt["t"])
                    wait("s", "v", bi_vm[1])
                    emit("s", lambda E_, ps=ps, q=q: [
                        E_.copy(
                            z2_ap[:, 0:8, q * 128:q * 128 + 128],
                            ps[:, 0:1024].rearrange("p (a b) -> p a b", b=128))],
                        incs=["s"])
                    psum_set([pname], "s", cnt["s"])

            def emit_A2(tt, half, dribble=None):
                wait("t", "G", wit_done)
                wait("t", "s", z_evac[tt * 4 + 3])
                dcs = list(range(half * 4, half * 4 + 4))
                gi = 0
                for rv in range(2):
                    w_sb = wrt_sb if rv == 0 else wit_sb
                    dst = r_ap if rv == 0 else i_ap
                    for j in dcs:
                        slot = (half * 8 + gi) % 4
                        pn, off = RIP[slot]
                        ps = PT[pn][:, off:off + 512]
                        if rip_guard[slot] is not None:
                            wait("t", rip_guard[slot][0], rip_guard[slot][1])
                        def rmm(E_, w_sb=w_sb, j=j, tt=tt, ps=ps):
                            insts = []
                            for k in range(8):
                                E_.ldweights(w_sb[:, k, j * 128:j * 128 + 128])
                                insts.append(E_.matmul(
                                    ps, w_sb[:, k, j * 128:j * 128 + 128],
                                    z_sb[:, k, tt * 512:tt * 512 + 512],
                                    start=(k == 0), stop=(k == 7)))
                            return insts
                        emit("t", rmm, incs=["t"])
                        wait("s", "t", cnt["t"])
                        wait("s", "v", scan_vm[half])
                        emit("s", lambda E_, ps=ps, dst=dst, j=j: [
                            E_.activation(dst[:, j, :], ps, AF.Sigmoid)],
                            incs=["s"])
                        rip_guard[slot] = ("s", cnt["s"])
                        gi += 1
                        if dribble is not None and gi % 2 == 0:
                            dribble[0].step(xdep=dribble[1])
                ri_sm = cnt["s"]
                wait("v", "s", ri_sm)
                wait("v", "v", l8_vm)
                for dc in dcs:
                    emit("v", lambda E_, dc=dc: [
                        E_.tensor_scalar(r_ap[:, dc, :], r_ap[:, dc, :],
                                         l8_sb[:, dc:dc + 1], None, ALU.mult)],
                        incs=["v"])
                la_vm = cnt["v"]
                for dc in dcs:
                    emit("v", lambda E_, dc=dc: [
                        E_.tensor_tensor(i_ap[:, dc, :], i_ap[:, dc, :],
                                         z2_ap[:, dc, :], ALU.mult)], incs=["v"])
                bi_vm[half] = cnt["v"]
                wait("s", "v", la_vm)
                wait("s", "v", bt_vm[half])
                for dc in dcs:
                    emit("s", lambda E_, dc=dc: [
                        E_.activation(a8_ap[:, dc, :], r_ap[:, dc, :], AF.Exp)],
                        incs=["s"])
                wait("s", "s", cnt["s"])
                for dc in dcs:
                    emit("s", lambda E_, dc=dc: [
                        E_.activation(r_ap[:, dc, :], r_ap[:, dc, :],
                                      AF.Exp, scale=2.0)], incs=["s"])
                wait("s", "s", cnt["s"])
                for dc in dcs:
                    emit("s", lambda E_, dc=dc: [
                        E_.activation(r_ap[:, dc, :], r_ap[:, dc, :],
                                      AF.Relu, scale=-1.0, bias=1.0)], incs=["s"])
                wait("s", "s", cnt["s"])
                for dc in dcs:
                    emit("s", lambda E_, dc=dc: [
                        E_.activation(r_ap[:, dc, :], r_ap[:, dc, :],
                                      AF.Sqrt)], incs=["s"])
                chain_sm[half] = cnt["s"]

            def emit_BV(tt, half):
                dcs = list(range(half * 4, half * 4 + 4))
                wait("v", "s", chain_sm[half])
                wait("v", "v", bi_vm[half])
                for dc in dcs:
                    emit("v", lambda E_, dc=dc: [
                        E_.tensor_tensor(i_ap[:, dc, :], i_ap[:, dc, :],
                                         r_ap[:, dc, :], ALU.mult)], incs=["v"])
                    wait("v", "v", cnt["v"])
                    init = 0.0 if tt == 0 else carry_sb[:, dc:dc + 1]
                    emit("v", lambda E_, dc=dc, init=init: [
                        E_.tensor_tensor_scan(a8_ap[:, dc, :], a8_ap[:, dc, :],
                                              i_ap[:, dc, :], init,
                                              ALU.mult, ALU.add)], incs=["v"])
                scan_vm[half] = cnt["v"]

            def emit_BT(tt, half):
                dcs = list(range(half * 4, half * 4 + 4))
                for dc in dcs:
                    pname = "pA" if dc % 2 == 0 else "pB"
                    ps = PT[pname]
                    wait("t", "v", scan_vm[half])
                    psum_wait("t", [pname])
                    def trh(E_, dc=dc, ps=ps):
                        insts = []
                        for qq in range(4):
                            insts.append(E_.transpose(
                                ps[:, qq * 128:qq * 128 + 128],
                                a8_ap[:, dc, qq * 128:qq * 128 + 128],
                                idf_sb[:, :]))
                        return insts
                    emit("t", trh, incs=["t"])
                    wait("v", "t", cnt["t"])
                    xdst = x_sb[:, tt * 4:tt * 4 + 4, dc * 128:dc * 128 + 128]
                    emit("v", lambda E_, ps=ps, xdst=xdst: [
                        E_.tensor_tensor(
                            xdst,
                            ps[:, 0:512].rearrange("p (a b) -> p a b", b=128),
                            xdst, ALU.add)])
                    wait("v", "v", scan_vm[half])
                    emit("v", lambda E_, dc=dc: [
                        E_.tensor_copy(carry_sb[:, dc:dc + 1],
                                       a8_ap[:, dc, 511:512])], incs=["v"])
                    psum_set([pname], "v", cnt["v"])
                bt_vm[half] = cnt["v"]

            for tt in range(5):
                if tt < 4:
                    emit_A1(tt)
                if tt >= 1:
                    emit_BV(tt - 1, 0)
                    emit_BT(tt - 1, 0)
                if tt < 4:
                    emit_A2(tt, 0)
                if tt >= 1:
                    emit_BV(tt - 1, 1)
                    emit_BT(tt - 1, 1)
                if tt < 4:
                    drb = None
                    if ln_tail is not None and tt >= 1:
                        drb = (ln_tail, ("v", bt_vm[1]))
                    emit_A2(tt, 1, dribble=drb)
            for slot in range(4):
                if rip_guard[slot] is not None:
                    pn = RIP[slot][0]
                    if pg[pn] is None or pg[pn][1] < rip_guard[slot][1]:
                        pg[pn] = rip_guard[slot]
            return cnt["v"], cnt["t"], cnt["s"]

        nexp = NE
        for s_ in stages:
            if str(s_).startswith("nexp:"):
                nexp = int(str(s_).split(":")[1])
        ffn1_tm = 0
        ln2 = None
        ln3 = None
        if "ffn1" in stages or dbg:
            ln1 = LNPipe(save_stats=False, zt0=ZT_F, zt1=ZT_F + 1024,
                         psA="pA", psB="pB")
            z1 = ln1.finish()
            if "dump_z" in stages:
                wait("q", "s", z1[-1])
                dma_emit("q", lambda E_: E_.dma_start(zdump_d, z_sb[:, :, :]))
            ltm1 = emit_gates(z1)
            wfin1 = emit_gate_math(ltm1)
            if dbg:
                wait("q", "v", wfin1)
                dma_emit("q", lambda E_: E_.dma_start(wfdump_d, wfin_sb[:, :, :]))
            if "ffn1" in stages:
                if "ssm" in stages:
                    ln2 = LNPipe(save_stats=True, zt0=ZT2_F, zt1=ZT2_F + 1024,
                                 psA="pB", psB="yB")
                ffn1_vm = emit_ffn(z1, wfin1, nexp=nexp, ln_tail=ln2)
                ffn1_tm = cnt["t"]
        ffn1_slices = slice_idx[0]

        ssm_vm = ssm_tm = ssm_sm = 0
        if "ssm" in stages:
            if ln2 is None:
                ln2 = LNPipe(save_stats=True, zt0=ZT2_F, zt1=ZT2_F + 1024,
                             psA="pB", psB="yB")
            z2 = ln2.finish()
            ln2_sm, ln2_vm = cnt["s"], cnt["v"]
            if "ffn3" in stages:
                ln3 = LNPipe(save_stats=False, zt0=ZT_F, zt1=ZT_F + 1024,
                             psA="pA", psB="pB")
            ssm_vm, ssm_tm, ssm_sm = emit_ssm(
                z2, ring_free_dep=ffn1_slices,
                w_deps=[("s", ln2_sm), ("v", ln2_vm)], ln_tail=ln3,
                xvm=cnt["v"])

        if "ffn3" in stages:
            if ln3 is None:
                ln3 = LNPipe(save_stats=False, zt0=ZT_F, zt1=ZT_F + 1024,
                             psA="pA", psB="pB")
            z3 = ln3.finish(xdep=("v", ssm_vm))
            ltm3 = emit_gates(z3)
            wfin3 = emit_gate_math(ltm3)
            wait("g", "t", ssm_tm)
            wait("g", "v", ssm_vm)
            wait("g", "s", ssm_sm)
            ffn3_vm = emit_ffn(z3, wfin3, nexp=nexp)

        wait("q", "v", cnt["v"])
        wait("q", "s", cnt["s"])
        wait("q", "t", cnt["t"])
        for c in range(4):
            dma_emit("q", lambda E_, c=c: E_.dma_start(
                out_d[:, 4 * c:4 * c + 4, :], x_sb[:, 4 * c:4 * c + 4, :]))
        ops["q"].append(lambda E_: E_.wait_ge(SEM["q"], dma["q"]))

        blk = E(nc.Block())

        @blk.sync
        def _(eng):
            for fn in ops["q"]:
                fn(eng)

        @blk.gpsimd
        def _(eng):
            for fn in ops["g"]:
                fn(eng)

        @blk.tensor
        def _(eng):
            for fn in ops["t"]:
                fn(eng)

        @blk.scalar
        def _(eng):
            for fn in ops["s"]:
                fn(eng)

        @blk.vector
        def _(eng):
            for fn in ops["v"]:
                fn(eng)

    return nc


_NC_CACHE = [None]


def kernel(**inputs):
    xs, shared = _host_layout(inputs)
    if _NC_CACHE[0] is None:
        _NC_CACHE[0] = build_nc()
    nc = _NC_CACHE[0]
    in_maps = []
    for b in range(8):
        m = {"x": xs[b]}
        m.update(shared)
        in_maps.append(m)
    res = run_bass_kernel_spmd(nc, in_maps, core_ids=list(range(8)))
    outs = []
    for b in range(8):
        o = np.asarray(res.results[b]["out"])
        outs.append(np.ascontiguousarray(o.transpose(1, 0, 2)).reshape(2048, 1024))
    return np.stack(outs).astype(np.float32)


# version 16
# speedup vs baseline: 1.1924x; 1.1924x over previous
import numpy as np
import ml_dtypes
from contextlib import ExitStack

import concourse.bass as bass
import concourse.mybir as mybir
from concourse.bass_utils import run_bass_kernel_spmd

F32 = mybir.dt.float32
BF16 = mybir.dt.bfloat16
AF = mybir.ActivationFunctionType
ALU = mybir.AluOpType
AX = mybir.AxisListType

D = 1024
T = 2048
I = 1408
NE = 10
NI = 11
NTC = 16
RING = 27
EPS = 1e-5

SCR_BF = 55296
ACT_OFF = 27648
SG_F = 25088
FIL_F = (ACT_OFF + 8 * 2048) // 2
ZT2_F = 3072
WRT_BF = 0
WIT_BF = 8192
R_F = 8192
I_F = 12288
Z2_F = 16384
LA_F = 20480
A_F = 22528
ZT_F = 24576


def _host_layout(inputs):
    g = {k: np.ascontiguousarray(np.asarray(v, dtype=np.float32))
         for k, v in inputs.items()}
    x = g["L0_attn_x"]

    def lay_up(w):
        return w.reshape(8, 128, 11, 128).transpose(2, 1, 0, 3)

    wg_l, wu_l, wd_l = [], [], []
    for e in range(8):
        wg_l.append(lay_up(g["L0_ffn_wg"][e]))
        wu_l.append(lay_up(g["L0_ffn_wu"][e]))
        wd_l.append(g["L0_ffn_wd"][e].reshape(11, 128, 1024))
    for h in range(2):
        wg_l.append(lay_up(g["L0_ffn_swg"][:, h * I:(h + 1) * I]))
        wu_l.append(lay_up(g["L0_ffn_swu"][:, h * I:(h + 1) * I]))
        wd_l.append(g["L0_ffn_swd"][h * I:(h + 1) * I, :].reshape(11, 128, 1024))
    wgs = np.ascontiguousarray(np.stack(wg_l))
    wus = np.ascontiguousarray(np.stack(wu_l))
    wds = np.ascontiguousarray(np.stack(wd_l))

    def lay_sq(w):
        return np.ascontiguousarray(w.reshape(8, 128, 1024).transpose(1, 0, 2))

    wrt = lay_sq(np.ascontiguousarray(g["L0_attn_W_r"].T))
    wit = lay_sq(np.ascontiguousarray(g["L0_attn_W_i"].T))
    gwt = np.ascontiguousarray(
        g["L0_ffn_gate_w"].T.reshape(8, 128, 8).transpose(1, 0, 2))
    lam = np.ascontiguousarray(g["L0_attn_lambd"].reshape(8, 128).T)
    idb = np.eye(128, dtype=ml_dtypes.bfloat16)
    idf = np.eye(128, dtype=np.float32)
    cst = np.tile(np.array([[EPS, 0.0, 0.0, 0.0]], dtype=np.float32), (128, 1))

    xs = [np.ascontiguousarray(x[b].reshape(16, 128, 1024).transpose(1, 0, 2))
          for b in range(8)]
    shared = dict(wgs=wgs, wus=wus, wds=wds, wrt=wrt, wit=wit, gwt=gwt,
                  lam=lam, idb=idb, idf=idf, cst=cst)
    return xs, shared


def build_nc(stages=("ffn1", "ssm", "ffn3")):
    nc = bass.Bass("TRN2", target_bir_lowering=False)

    x_d = nc.dram_tensor("x", [128, 16, 1024], F32, kind="ExternalInput").ap()
    wgs_d = nc.dram_tensor("wgs", [NE, NI, 128, 8, 128], F32, kind="ExternalInput").ap()
    wus_d = nc.dram_tensor("wus", [NE, NI, 128, 8, 128], F32, kind="ExternalInput").ap()
    wds_d = nc.dram_tensor("wds", [NE, NI, 128, 1024], F32, kind="ExternalInput").ap()
    wrt_d = nc.dram_tensor("wrt", [128, 8, 1024], F32, kind="ExternalInput").ap()
    wit_d = nc.dram_tensor("wit", [128, 8, 1024], F32, kind="ExternalInput").ap()
    gwt_d = nc.dram_tensor("gwt", [128, 8, 8], F32, kind="ExternalInput").ap()
    lam_d = nc.dram_tensor("lam", [128, 8], F32, kind="ExternalInput").ap()
    idb_d = nc.dram_tensor("idb", [128, 128], BF16, kind="ExternalInput").ap()
    idf_d = nc.dram_tensor("idf", [128, 128], F32, kind="ExternalInput").ap()
    cst_d = nc.dram_tensor("cst", [128, 4], F32, kind="ExternalInput").ap()
    out_d = nc.dram_tensor("out", [128, 16, 1024], F32, kind="ExternalOutput").ap()
    dbg = any(str(s).startswith("dump") or str(s).startswith("nexp") for s in stages)
    if dbg:
        zdump_d = nc.dram_tensor("zdump", [128, 8, 2048], BF16, kind="ExternalOutput").ap()
        wfdump_d = nc.dram_tensor("wfdump", [128, 16, 8], F32, kind="ExternalOutput").ap()
    if "dump_ssm" in stages:
        sdump_d = nc.dram_tensor("sdump", [128, 5, 8, 512], F32, kind="ExternalOutput").ap()
        ldump_d = nc.dram_tensor("ldump", [128, 16, 2], F32, kind="ExternalOutput").ap()

    with ExitStack() as stack:
        E = stack.enter_context
        x_sb = E(nc.sbuf_tensor("x_sb", [128, 16, 1024], F32))
        z_sb = E(nc.sbuf_tensor("z_sb", [128, 8, 2048], BF16))
        scr = E(nc.sbuf_tensor("scr", [128, SCR_BF], BF16))
        gwt_sb = E(nc.sbuf_tensor("gwt_sb", [128, 8, 8], BF16))
        lam_sb = E(nc.sbuf_tensor("lam_sb", [128, 8], F32))
        l8_sb = E(nc.sbuf_tensor("l8_sb", [128, 8], F32))
        cst_sb = E(nc.sbuf_tensor("cst_sb", [128, 4], F32))
        idb_sb = E(nc.sbuf_tensor("idb_sb", [128, 128], BF16))
        idf_sb = E(nc.sbuf_tensor("idf_sb", [128, 128], F32))
        wfin_sb = E(nc.sbuf_tensor("wfin_sb", [128, 16, 8], F32))
        wfin_fl = wfin_sb.reshape([128, 128])
        ltm_sb = E(nc.sbuf_tensor("ltm_sb", [128, 16, 8], F32))
        lgt_sb = E(nc.sbuf_tensor("lgt_sb", [128, 16, 8], F32))
        stat_sb = E(nc.sbuf_tensor("stat_sb", [128, 48], F32))
        ln2_sb = E(nc.sbuf_tensor("ln2_sb", [128, 16, 2], F32))
        ln2_fl = ln2_sb.reshape([128, 32])
        carry_sb = E(nc.sbuf_tensor("carry_sb", [128, 8], F32))
        gmax_sb = E(nc.sbuf_tensor("gmax_sb", [128, 16], F32))
        gm2_sb = E(nc.sbuf_tensor("gm2_sb", [128, 16], F32))
        gden_sb = E(nc.sbuf_tensor("gden_sb", [128, 16], F32))

        PT = {
            "pA": E(nc.psum_tensor("pA", [128, 1024], F32)),
            "pB": E(nc.psum_tensor("pB", [128, 1024], F32)),
            "yA": E(nc.psum_tensor("yA", [128, 1024], F32)),
            "yB": E(nc.psum_tensor("yB", [128, 1024], F32)),
        }

        SEM = {
            "t": E(nc.semaphore("sem_t")),
            "v": E(nc.semaphore("sem_v")),
            "s": E(nc.semaphore("sem_s")),
            "g": E(nc.semaphore("sem_g")),
            "q": E(nc.semaphore("sem_q")),
            "f": E(nc.semaphore("sem_f")),
            "G": E(nc.semaphore("sem_G")),
            "Q": E(nc.semaphore("sem_Q")),
        }

        scrf = scr.bitcast(F32)

        ops = {k: [] for k in "tvsgq"}
        cnt = {"t": 0, "v": 0, "s": 0, "f": 0, "G": 0, "Q": 0}
        dma = {"g": 0, "q": 0}
        known = {k: {} for k in "tvsgq"}

        def wait(eng, key, val):
            if val is None or val <= 0:
                return
            if val <= known[eng].get(key, 0):
                return
            known[eng][key] = val
            ops[eng].append(lambda E_, s=SEM[key], v=val: E_.wait_ge(s, v))

        def emit(eng, fn, incs=()):
            norm = [(k, -1) if isinstance(k, str) else k for k in incs]
            attach = [(k, i) for k, i in norm if i is not None]
            post = [k for k, i in norm if i is None]
            idxs = [i for _, i in attach]
            assert len(set(idxs)) == len(idxs)
            own = cnt[eng] + sum(1 for k, i in attach if k == eng)
            def run(E_, fn=fn, attach=tuple(attach), post=tuple(post), own=own):
                insts = fn(E_)
                for k, idx in attach:
                    insts[idx].then_inc(SEM[k], 1)
                for k in post:
                    E_.wait_ge(SEM[eng], own).then_inc(SEM[k], 1)
            ops[eng].append(run)
            for k, _ in norm:
                cnt[k] += 1

        def dma_emit(eng, fn):
            def run(E_, fn=fn):
                fn(E_).then_inc(SEM[eng], 16)
            ops[eng].append(run)
            dma[eng] += 16
            return dma[eng]

        def dma_flush(eng):
            pub = "G" if eng == "g" else "Q"
            v = dma[eng]
            def run(E_, v=v):
                E_.wait_ge(SEM[eng], v)
                E_.nop().then_inc(SEM[pub], 1)
            ops[eng].append(run)
            cnt[pub] += 1
            return cnt[pub]

        pg = {k: None for k in PT}
        ztg_map = {}
        sqm_map = {}
        sgg = [None, None]

        def map_wait(eng, mp, key):
            g = mp.get(key)
            if g is not None:
                wait(eng, g[0], g[1])

        def res_wait(eng, guards, idxs):
            for n in idxs:
                g = guards[n]
                if g is not None:
                    wait(eng, g[0], g[1])

        def psum_wait(eng, names):
            for n in names:
                if pg[n] is not None:
                    wait(eng, pg[n][0], pg[n][1])

        def psum_set(names, key, val):
            for n in names:
                pg[n] = (key, val)

        def ring_slot(k):
            s0 = (k % RING) * 1024
            return scr[:, s0:s0 + 1024]

        def act_ap(i):
            o = ACT_OFF + i * 2048
            return scr[:, o:o + 2048]

        def sg_ap(p):
            o = SG_F + p * 1024
            return scrf[:, o:o + 1024]

        def zt_ap(p):
            o = ZT_F + p * 1024
            return scrf[:, o:o + 1024]

        def zt_bf(p):
            o = 2 * (ZT_F + p * 1024)
            return scr[:, o:o + 1024]

        def fil_ap():
            return scrf[:, FIL_F:FIL_F + 512]

        x_ready = []
        for c in range(8):
            dma_emit("q", lambda E_, c=c: E_.dma_start(
                x_sb[:, 2 * c:2 * c + 2, :], x_d[:, 2 * c:2 * c + 2, :]))
            x_ready.append(dma_flush("q"))
        dma_emit("q", lambda E_: E_.dma_start(cst_sb[:, :], cst_d))
        dma_emit("q", lambda E_: E_.dma_start(lam_sb[:, :], lam_d))
        dma_emit("q", lambda E_: E_.dma_start(idb_sb[:, :], idb_d))
        dma_emit("q", lambda E_: E_.dma_start(idf_sb[:, :], idf_d))
        cst_ready = lam_ready = idb_ready = idf_ready = dma_flush("q")
        dma_emit("g", lambda E_: E_.dma_start(gwt_sb[:, :, :], gwt_d))
        gwt_ready = dma_flush("g")

        wait("s", "Q", lam_ready)
        emit("s", lambda E_: [E_.activation(l8_sb[:, :], lam_sb[:, :], AF.Sigmoid)],
             incs=["s"])
        wait("s", "s", cnt["s"])
        emit("s", lambda E_: [E_.activation(l8_sb[:, :], l8_sb[:, :], AF.Ln)],
             incs=["s"])
        l8_sm = cnt["s"]
        wait("v", "s", l8_sm)
        emit("v", lambda E_: [E_.tensor_scalar(l8_sb[:, :], l8_sb[:, :], 8.0,
                                               None, ALU.mult)], incs=["v"])
        l8_vm = cnt["v"]

        class LNPipe:
            def __init__(self, save_stats, zt0, zt1, psA, psB):
                self.save = save_stats
                self.zt = (zt0, zt1)
                self.ps = (psA, psB)
                self.k = 0
                self.z_evac = []
                self.sq_sm = [0] * 6
                self.red_vm = [0] * 6
                self.m_sm = [0] * 6
                self.nvs_vm = [0] * 6
                self.sd_sm = [0] * 6
                self.rstd_vm = [0] * 6

            def zt_f(self, p):
                o = self.zt[p]
                return scrf[:, o:o + 1024]

            def zt_b(self, p):
                o = 2 * self.zt[p]
                return scr[:, o:o + 1024]

            def step(self, xdep=None):
                k = self.k
                self.k += 1
                if k < NTC:
                    tc = k
                    par = (k % 6) * 8
                    sl = k % 2
                    off = self.zt[sl]
                    xc = x_sb[:, tc, :]
                    wait("s", "Q", x_ready[tc // 2])
                    wait("s", "Q", cst_ready)
                    if xdep is not None:
                        wait("s", xdep[0], xdep[1])
                    map_wait("s", ztg_map, off)
                    dsc = self.zt_f(sl)
                    emit("s", lambda E_, xc=xc, dsc=dsc, par=par: [
                        E_.activation(dsc, xc, AF.Square,
                                      accum_out=stat_sb[:, par + 1:par + 2])],
                        incs=["s"])
                    sqm_map[off] = ("s", cnt["s"])
                    self.sq_sm[k % 6] = cnt["s"]
                    wait("v", "Q", x_ready[tc // 2])
                    if xdep is not None:
                        wait("v", xdep[0], xdep[1])
                    emit("v", lambda E_, xc=xc, par=par: [
                        E_.reduce_sum(stat_sb[:, par + 0:par + 1], xc, AX.X)],
                        incs=["v"])
                    self.red_vm[k % 6] = cnt["v"]
                if 1 <= k <= NTC:
                    p1 = ((k - 1) % 6) * 8
                    wait("s", "v", self.red_vm[(k - 1) % 6])
                    wait("s", "s", self.sq_sm[(k - 1) % 6])
                    emit("s", lambda E_, p1=p1: [
                        E_.mul(stat_sb[:, p1 + 2:p1 + 3],
                               stat_sb[:, p1 + 0:p1 + 1], 1.0 / 1024.0)])
                    emit("s", lambda E_, p1=p1: [
                        E_.mul(stat_sb[:, p1 + 3:p1 + 4],
                               stat_sb[:, p1 + 1:p1 + 2], 1.0 / 1024.0)],
                        incs=["s"])
                    self.m_sm[(k - 1) % 6] = cnt["s"]
                if 2 <= k <= NTC + 1:
                    p2 = ((k - 2) % 6) * 8
                    wait("v", "s", self.m_sm[(k - 2) % 6])
                    emit("v", lambda E_, p2=p2: [
                        E_.scalar_tensor_tensor(
                            stat_sb[:, p2 + 4:p2 + 5],
                            stat_sb[:, p2 + 2:p2 + 3],
                            stat_sb[:, p2 + 2:p2 + 3],
                            stat_sb[:, p2 + 3:p2 + 4],
                            ALU.mult, ALU.subtract)], incs=["v"])
                    self.nvs_vm[(k - 2) % 6] = cnt["v"]
                if 3 <= k <= NTC + 2:
                    p3 = ((k - 3) % 6) * 8
                    wait("s", "v", self.nvs_vm[(k - 3) % 6])
                    emit("s", lambda E_, p3=p3: [
                        E_.activation(stat_sb[:, p3 + 5:p3 + 6],
                                      stat_sb[:, p3 + 4:p3 + 5], AF.Sqrt,
                                      scale=-1.0, bias=cst_sb[:, 0:1])],
                        incs=["s"])
                    self.sd_sm[(k - 3) % 6] = cnt["s"]
                if 4 <= k <= NTC + 3:
                    p4 = ((k - 4) % 6) * 8
                    wait("v", "s", self.sd_sm[(k - 4) % 6])
                    emit("v", lambda E_, p4=p4: [
                        E_.reciprocal(stat_sb[:, p4 + 6:p4 + 7],
                                      stat_sb[:, p4 + 5:p4 + 6])], incs=["v"])
                    self.rstd_vm[(k - 4) % 6] = cnt["v"]
                if 5 <= k:
                    tc = k - 5
                    p5 = (tc % 6) * 8
                    sl2 = tc % 2
                    off2 = self.zt[sl2]
                    wait("v", "v", self.rstd_vm[tc % 6])
                    if self.save:
                        emit("v", lambda E_, p5=p5, tc=tc: [
                            E_.tensor_copy(ln2_fl[:, tc * 2:tc * 2 + 1],
                                           stat_sb[:, p5 + 2:p5 + 3])])
                        emit("v", lambda E_, p5=p5, tc=tc: [
                            E_.tensor_copy(ln2_fl[:, tc * 2 + 1:tc * 2 + 2],
                                           stat_sb[:, p5 + 6:p5 + 7])])
                    ztm = self.zt_b(sl2)
                    map_wait("v", sqm_map, off2)
                    map_wait("v", ztg_map, off2)
                    emit("v", lambda E_, tc=tc, p5=p5, ztm=ztm: [
                        E_.tensor_scalar(ztm, x_sb[:, tc, :],
                                         stat_sb[:, p5 + 2:p5 + 3],
                                         stat_sb[:, p5 + 6:p5 + 7],
                                         ALU.subtract, ALU.mult)], incs=["v"])
                    ztm_vm = cnt["v"]
                    pname = self.ps[sl2]
                    psb = PT[pname].bitcast(BF16)
                    wait("t", "v", ztm_vm)
                    wait("t", "Q", idb_ready)
                    psum_wait("t", [pname])
                    def tr8(E_, ztm=ztm, psb=psb):
                        insts = []
                        for dc in range(8):
                            insts.append(E_.transpose(
                                psb[:, dc * 128:dc * 128 + 128],
                                ztm[:, dc * 128:dc * 128 + 128], idb_sb[:, :]))
                        return insts
                    emit("t", tr8, incs=["t"])
                    ztg_map[off2] = ("t", cnt["t"])
                    wait("s", "t", cnt["t"])
                    emit("s", lambda E_, psb=psb, tc=tc: [
                        E_.copy(
                            z_sb[:, 0:8, tc * 128:tc * 128 + 128],
                            psb[:, 0:1024].rearrange("p (a b) -> p a b", b=128))],
                        incs=["s"])
                    psum_set([pname], "s", cnt["s"])
                    self.z_evac.append(cnt["s"])

            def finish(self, xdep=None):
                while self.k < NTC + 5:
                    self.step(xdep=xdep)
                return self.z_evac

        def emit_gates(z_evac):
            wait("t", "G", gwt_ready)
            psum_wait("t", ["pA"])
            for tc in range(NTC):
                wait("t", "s", z_evac[tc])
                def gmm(E_, tc=tc):
                    insts = []
                    for kc in range(8):
                        E_.ldweights(z_sb[:, kc, tc * 128:tc * 128 + 128])
                        insts.append(E_.matmul(
                            PT["pA"][:, tc * 8:tc * 8 + 8],
                            z_sb[:, kc, tc * 128:tc * 128 + 128],
                            gwt_sb[:, kc, :],
                            start=(kc == 0), stop=(kc == 7)))
                    return insts
                emit("t", gmm, incs=["t"])
            wait("v", "t", cnt["t"])
            emit("v", lambda E_: [
                E_.tensor_copy(ltm_sb[:, :, :],
                               PT["pA"][:, 0:128].rearrange(
                                   "p (a b) -> p a b", b=8))], incs=["v"])
            psum_set(["pA"], "v", cnt["v"])
            return cnt["v"]

        def emit_gate_math(ltm_vm):
            l3 = ltm_sb[:, :, :]
            p3 = lgt_sb[:, :, :]
            t3 = wfin_sb[:, :, :]
            m1b = gmax_sb[:, :].unsqueeze(2).to_broadcast([128, 16, 8])
            m2b = gm2_sb[:, :].unsqueeze(2).to_broadcast([128, 16, 8])
            rdb = gden_sb[:, :].unsqueeze(2).to_broadcast([128, 16, 8])
            wait("v", "v", ltm_vm)
            emit("v", lambda E_: [E_.tensor_reduce(gmax_sb[:, :], l3, AX.X, ALU.max)],
                 incs=["v"])
            wait("v", "v", cnt["v"])
            emit("v", lambda E_: [E_.tensor_tensor(p3, l3, m1b, ALU.subtract)],
                 incs=["v"])
            wait("s", "v", cnt["v"])
            emit("s", lambda E_: [E_.activation(p3, p3, AF.Exp)], incs=["s"])
            emit("v", lambda E_: [E_.tensor_tensor(t3, l3, m1b, ALU.is_ge)],
                 incs=["v"])
            wait("v", "v", cnt["v"])
            emit("v", lambda E_: [E_.scalar_tensor_tensor(t3, t3, -1e30, l3,
                                                          ALU.mult, ALU.add)],
                 incs=["v"])
            wait("v", "v", cnt["v"])
            emit("v", lambda E_: [E_.tensor_reduce(gm2_sb[:, :], t3, AX.X, ALU.max)],
                 incs=["v"])
            wait("v", "v", cnt["v"])
            emit("v", lambda E_: [E_.tensor_tensor(t3, l3, m2b, ALU.is_ge)],
                 incs=["v"])
            wait("v", "v", cnt["v"])
            wait("v", "s", cnt["s"])
            emit("v", lambda E_: [E_.tensor_tensor(p3, p3, t3, ALU.mult)],
                 incs=["v"])
            wait("v", "v", cnt["v"])
            emit("v", lambda E_: [E_.tensor_reduce(gden_sb[:, :], p3, AX.X, ALU.add)],
                 incs=["v"])
            wait("v", "v", cnt["v"])
            emit("v", lambda E_: [E_.reciprocal(gden_sb[:, :], gden_sb[:, :])],
                 incs=["v"])
            wait("v", "v", cnt["v"])
            emit("v", lambda E_: [E_.tensor_tensor(p3, p3, rdb, ALU.mult)],
                 incs=["v"])
            wait("v", "v", cnt["v"])
            emit("v", lambda E_: [E_.tensor_scalar(wfin_sb[:, :, :], p3, 0.5,
                                                   None, ALU.mult)], incs=["v"])
            return cnt["v"]

        slice_idx = [0]
        unit_hist = []

        def ring_dma(src_ap, flush=True):
            k = slice_idx[0]
            wait("g", "f", k - RING + 1)
            dst = ring_slot(k)
            dma_emit("g", lambda E_, dst=dst, src=src_ap: E_.dma_start(dst, src))
            slice_idx[0] += 1
            v = dma_flush("g") if flush else None
            return k, v

        ymark = [0] * NTC

        def emit_ffn(z_evac, wfin_vm, nexp=NE, e0=0, ln_tail=None):
            z_all = z_evac[-1]
            for e in range(e0, nexp):
                wg_slots, wu_slots, wd_slots = [], [], []
                wg_done, wu_done, wd_done = [], [], []
                for i in range(NI):
                    k, _ = ring_dma(wgs_d[e, i], flush=False)
                    wg_slots.append(k)
                    k, v = ring_dma(wus_d[e, i], flush=True)
                    wu_slots.append(k); wu_done.append(v)
                for i in range(NI):
                    k, v = ring_dma(wds_d[e, i], flush=(i == NI - 1))
                    wd_slots.append(k)
                    if v is not None:
                        wd_done.append(v)

                for i in range(NI):
                    for tp in range(2):
                        par = (i * 2 + tp) % 2
                        hg_n = "pA" if par == 0 else "yA"
                        hu_n = "pB" if par == 0 else "yB"
                        hg, hu = PT[hg_n], PT[hu_n]
                        wait("t", "G", wu_done[i])
                        wait("t", "s", z_evac[7] if tp == 0 else z_evac[15])
                        psum_wait("t", [hg_n, hu_n])
                        wg_ap = ring_slot(wg_slots[i]).rearrange(
                            "p (a b) -> p a b", b=128)
                        wu_ap = ring_slot(wu_slots[i]).rearrange(
                            "p (a b) -> p a b", b=128)
                        t0 = tp * 1024
                        incs = (["t"] if tp == 0
                                else [("t", -1), ("f", 15), ("f", None)])
                        def up(E_, wg_ap=wg_ap, wu_ap=wu_ap, hg=hg, hu=hu, t0=t0):
                            insts = []
                            for d in range(8):
                                E_.ldweights(wg_ap[:, d, :])
                                insts.append(E_.matmul(
                                    hg[:, 0:512], wg_ap[:, d, :],
                                    z_sb[:, d, t0:t0 + 512],
                                    start=(d == 0), stop=(d == 7)))
                                insts.append(E_.matmul(
                                    hg[:, 512:1024], wg_ap[:, d, :],
                                    z_sb[:, d, t0 + 512:t0 + 1024],
                                    start=(d == 0), stop=(d == 7)))
                            for d in range(8):
                                E_.ldweights(wu_ap[:, d, :])
                                insts.append(E_.matmul(
                                    hu[:, 0:512], wu_ap[:, d, :],
                                    z_sb[:, d, t0:t0 + 512],
                                    start=(d == 0), stop=(d == 7)))
                                insts.append(E_.matmul(
                                    hu[:, 512:1024], wu_ap[:, d, :],
                                    z_sb[:, d, t0 + 512:t0 + 1024],
                                    start=(d == 0), stop=(d == 7)))
                            return insts
                        emit("t", up, incs=incs)
                        tmark = cnt["t"]
                        wait("s", "t", tmark)
                        res_wait("s", sgg, [par])
                        emit("s", lambda E_, hg=hg, par=par: [
                            E_.activation(sg_ap(par), hg[:, :], AF.Silu)],
                            incs=["s"])
                        wait("v", "s", cnt["s"])
                        dst = act_ap(i)[:, tp * 1024:tp * 1024 + 1024]
                        emit("v", lambda E_, hu=hu, par=par, dst=dst: [
                            E_.tensor_tensor(dst, hu[:, :], sg_ap(par), ALU.mult)],
                            incs=["v"])
                        vm = cnt["v"]
                        psum_set([hg_n, hu_n], "v", vm)
                        sgg[par] = ("v", vm)
                acts_vm = cnt["v"]

                for tc in range(NTC):
                    y_n = "pA" if tc % 2 == 0 else "yA"
                    yps = PT[y_n]
                    wait("t", "G", wd_done[-1])
                    wait("t", "v", acts_vm)
                    psum_wait("t", [y_n])
                    last_group = (tc == NTC - 1)
                    if last_group:
                        incs = ([("t", -1)]
                                + [("f", 2 * i + 2) for i in range(NI - 1)]
                                + [("f", None)])
                    else:
                        incs = ["t"]
                    def down(E_, tc=tc, yps=yps, wd_slots=tuple(wd_slots)):
                        insts = []
                        for i in range(NI):
                            a_ap = act_ap(i)[:, tc * 128:tc * 128 + 128]
                            w_ap = ring_slot(wd_slots[i])
                            E_.ldweights(a_ap)
                            insts.append(E_.matmul(
                                yps[:, 0:512], a_ap, w_ap[:, 0:512],
                                start=(i == 0), stop=(i == NI - 1)))
                            insts.append(E_.matmul(
                                yps[:, 512:1024], a_ap, w_ap[:, 512:1024],
                                start=(i == 0), stop=(i == NI - 1)))
                        return insts
                    emit("t", down, incs=incs)
                    wait("v", "t", cnt["t"])
                    wait("v", "v", ymark[tc])
                    if e < 8:
                        wait("v", "v", wfin_vm)
                        sc_ap = wfin_fl[:, tc * 8 + e:tc * 8 + e + 1]
                        emit("v", lambda E_, yps=yps, tc=tc, sc_ap=sc_ap: [
                            E_.scalar_tensor_tensor(
                                x_sb[:, tc, :], yps[:, :], sc_ap,
                                x_sb[:, tc, :], ALU.mult, ALU.add)], incs=["v"])
                    else:
                        emit("v", lambda E_, yps=yps, tc=tc: [
                            E_.scalar_tensor_tensor(
                                x_sb[:, tc, :], yps[:, :], 0.5,
                                x_sb[:, tc, :], ALU.mult, ALU.add)], incs=["v"])
                    ymark[tc] = cnt["v"]
                    psum_set([y_n], "v", cnt["v"])
                    if ln_tail is not None and e == nexp - 1:
                        ln_tail.step(xdep=("v", cnt["v"]))
            return cnt["v"]

        def emit_ssm(z_evac, ring_free_dep, w_deps=(), ln_tail=None, xvm=0):
            wait("g", "f", ring_free_dep)
            for key_, val_ in w_deps:
                wait("g", key_, val_)
            wrt_sb = scr[:, WRT_BF:WRT_BF + 8192].rearrange("p (a b) -> p a b", b=1024)
            wit_sb = scr[:, WIT_BF:WIT_BF + 8192].rearrange("p (a b) -> p a b", b=1024)
            dma_emit("g", lambda E_: E_.dma_start(wrt_sb, wrt_d))
            dma_emit("g", lambda E_: E_.dma_start(wit_sb, wit_d))
            wit_done = dma_flush("g")

            r_ap = scrf[:, R_F:R_F + 4096].rearrange("p (a b) -> p a b", b=512)
            i_ap = scrf[:, I_F:I_F + 4096].rearrange("p (a b) -> p a b", b=512)
            z2_ap = scrf[:, Z2_F:Z2_F + 4096].rearrange("p (a b) -> p a b", b=512)
            a8_ap = scrf[:, LA_F:LA_F + 4096].rearrange("p (a b) -> p a b", b=512)

            RIP = [("yA", 0), ("yA", 512), ("yB", 0), ("yB", 512)]
            rip_guard = [pg["yA"], pg["yA"], pg["yB"], pg["yB"]]

            z_all = z_evac[-1]
            bi_vm = [0, 0]
            chain_sm = [0, 0]
            scan_vm = [0, 0]
            bt_vm = [0, 0]

            def emit_A1(tt):
                for q in range(4):
                    tc = tt * 4 + q
                    par = q % 2
                    zt = zt_ap(par)
                    mcol = ln2_fl[:, tc * 2:tc * 2 + 1]
                    rstd = ln2_fl[:, tc * 2 + 1:tc * 2 + 2]
                    zoff = ZT_F + par * 1024
                    map_wait("v", ztg_map, zoff)
                    map_wait("v", sqm_map, zoff)
                    wait("v", "v", xvm)
                    emit("v", lambda E_, tc=tc, zt=zt, mcol=mcol, rstd=rstd: [
                        E_.tensor_scalar(zt, x_sb[:, tc, :], mcol, rstd,
                                         ALU.subtract, ALU.mult)], incs=["v"])
                    pname = "pA" if par == 0 else "pB"
                    ps = PT[pname]
                    wait("t", "v", cnt["v"])
                    wait("t", "Q", idf_ready)
                    psum_wait("t", [pname])
                    def trz(E_, zt=zt, ps=ps):
                        insts = []
                        for dc in range(8):
                            insts.append(E_.transpose(
                                ps[:, dc * 128:dc * 128 + 128],
                                zt[:, dc * 128:dc * 128 + 128], idf_sb[:, :]))
                        return insts
                    emit("t", trz, incs=["t"])
                    ztg_map[zoff] = ("t", cnt["t"])
                    wait("s", "t", cnt["t"])
                    wait("s", "v", bi_vm[1])
                    emit("s", lambda E_, ps=ps, q=q: [
                        E_.copy(
                            z2_ap[:, 0:8, q * 128:q * 128 + 128],
                            ps[:, 0:1024].rearrange("p (a b) -> p a b", b=128))],
                        incs=["s"])
                    psum_set([pname], "s", cnt["s"])

            def emit_A2(tt, half, dribble=None):
                wait("t", "G", wit_done)
                wait("t", "s", z_evac[tt * 4 + 3])
                dcs = list(range(half * 4, half * 4 + 4))
                gi = 0
                for rv in range(2):
                    w_sb = wrt_sb if rv == 0 else wit_sb
                    dst = r_ap if rv == 0 else i_ap
                    for j in dcs:
                        slot = (half * 8 + gi) % 4
                        pn, off = RIP[slot]
                        ps = PT[pn][:, off:off + 512]
                        if rip_guard[slot] is not None:
                            wait("t", rip_guard[slot][0], rip_guard[slot][1])
                        def rmm(E_, w_sb=w_sb, j=j, tt=tt, ps=ps):
                            insts = []
                            for k in range(8):
                                E_.ldweights(w_sb[:, k, j * 128:j * 128 + 128])
                                insts.append(E_.matmul(
                                    ps, w_sb[:, k, j * 128:j * 128 + 128],
                                    z_sb[:, k, tt * 512:tt * 512 + 512],
                                    start=(k == 0), stop=(k == 7)))
                            return insts
                        emit("t", rmm, incs=["t"])
                        wait("s", "t", cnt["t"])
                        wait("s", "v", scan_vm[half])
                        emit("s", lambda E_, ps=ps, dst=dst, j=j: [
                            E_.activation(dst[:, j, :], ps, AF.Sigmoid)],
                            incs=["s"])
                        rip_guard[slot] = ("s", cnt["s"])
                        gi += 1
                        if dribble is not None and gi % 2 == 0:
                            dribble[0].step(xdep=dribble[1])
                ri_sm = cnt["s"]
                wait("v", "s", ri_sm)
                wait("v", "v", l8_vm)
                for dc in dcs:
                    emit("v", lambda E_, dc=dc: [
                        E_.tensor_scalar(r_ap[:, dc, :], r_ap[:, dc, :],
                                         l8_sb[:, dc:dc + 1], None, ALU.mult)],
                        incs=["v"])
                la_vm = cnt["v"]
                for dc in dcs:
                    emit("v", lambda E_, dc=dc: [
                        E_.tensor_tensor(i_ap[:, dc, :], i_ap[:, dc, :],
                                         z2_ap[:, dc, :], ALU.mult)], incs=["v"])
                bi_vm[half] = cnt["v"]
                wait("s", "v", la_vm)
                wait("s", "v", bt_vm[half])
                for dc in dcs:
                    emit("s", lambda E_, dc=dc: [
                        E_.activation(a8_ap[:, dc, :], r_ap[:, dc, :], AF.Exp)],
                        incs=["s"])
                wait("s", "s", cnt["s"])
                for dc in dcs:
                    emit("s", lambda E_, dc=dc: [
                        E_.activation(r_ap[:, dc, :], r_ap[:, dc, :],
                                      AF.Exp, scale=2.0)], incs=["s"])
                wait("s", "s", cnt["s"])
                for dc in dcs:
                    emit("s", lambda E_, dc=dc: [
                        E_.activation(r_ap[:, dc, :], r_ap[:, dc, :],
                                      AF.Relu, scale=-1.0, bias=1.0)], incs=["s"])
                wait("s", "s", cnt["s"])
                for dc in dcs:
                    emit("s", lambda E_, dc=dc: [
                        E_.activation(r_ap[:, dc, :], r_ap[:, dc, :],
                                      AF.Sqrt)], incs=["s"])
                chain_sm[half] = cnt["s"]

            def emit_BV(tt, half):
                dcs = list(range(half * 4, half * 4 + 4))
                wait("v", "s", chain_sm[half])
                wait("v", "v", bi_vm[half])
                for dc in dcs:
                    emit("v", lambda E_, dc=dc: [
                        E_.tensor_tensor(i_ap[:, dc, :], i_ap[:, dc, :],
                                         r_ap[:, dc, :], ALU.mult)], incs=["v"])
                    wait("v", "v", cnt["v"])
                    init = 0.0 if tt == 0 else carry_sb[:, dc:dc + 1]
                    emit("v", lambda E_, dc=dc, init=init: [
                        E_.tensor_tensor_scan(a8_ap[:, dc, :], a8_ap[:, dc, :],
                                              i_ap[:, dc, :], init,
                                              ALU.mult, ALU.add)], incs=["v"])
                scan_vm[half] = cnt["v"]

            def emit_BT(tt, half):
                dcs = list(range(half * 4, half * 4 + 4))
                for dc in dcs:
                    pname = "pA" if dc % 2 == 0 else "pB"
                    ps = PT[pname]
                    wait("t", "v", scan_vm[half])
                    psum_wait("t", [pname])
                    def trh(E_, dc=dc, ps=ps):
                        insts = []
                        for qq in range(4):
                            insts.append(E_.transpose(
                                ps[:, qq * 128:qq * 128 + 128],
                                a8_ap[:, dc, qq * 128:qq * 128 + 128],
                                idf_sb[:, :]))
                        return insts
                    emit("t", trh, incs=["t"])
                    wait("v", "t", cnt["t"])
                    xdst = x_sb[:, tt * 4:tt * 4 + 4, dc * 128:dc * 128 + 128]
                    emit("v", lambda E_, ps=ps, xdst=xdst: [
                        E_.tensor_tensor(
                            xdst,
                            ps[:, 0:512].rearrange("p (a b) -> p a b", b=128),
                            xdst, ALU.add)])
                    wait("v", "v", scan_vm[half])
                    emit("v", lambda E_, dc=dc: [
                        E_.tensor_copy(carry_sb[:, dc:dc + 1],
                                       a8_ap[:, dc, 511:512])], incs=["v"])
                    psum_set([pname], "v", cnt["v"])
                bt_vm[half] = cnt["v"]

            for tt in range(5):
                if tt < 4:
                    emit_A1(tt)
                if tt >= 1:
                    emit_BV(tt - 1, 0)
                    emit_BT(tt - 1, 0)
                if tt < 4:
                    emit_A2(tt, 0)
                if tt >= 1:
                    emit_BV(tt - 1, 1)
                    emit_BT(tt - 1, 1)
                if tt < 4:
                    drb = None
                    if ln_tail is not None and tt >= 1:
                        drb = (ln_tail, ("v", bt_vm[1]))
                    emit_A2(tt, 1, dribble=drb)
            for slot in range(4):
                if rip_guard[slot] is not None:
                    pn = RIP[slot][0]
                    if pg[pn] is None or pg[pn][1] < rip_guard[slot][1]:
                        pg[pn] = rip_guard[slot]
            return cnt["v"], cnt["t"], cnt["s"]

        nexp = NE
        for s_ in stages:
            if str(s_).startswith("nexp:"):
                nexp = int(str(s_).split(":")[1])
        ffn1_tm = 0
        ln2 = None
        ln3 = None
        if "ffn1" in stages or dbg:
            ln1 = LNPipe(save_stats=False, zt0=ZT_F, zt1=ZT_F + 1024,
                         psA="pA", psB="pB")
            z1 = ln1.finish()
            if "dump_z" in stages:
                wait("q", "s", z1[-1])
                dma_emit("q", lambda E_: E_.dma_start(zdump_d, z_sb[:, :, :]))
            ltm1 = emit_gates(z1)
            wfin1 = emit_gate_math(ltm1)
            if dbg:
                wait("q", "v", wfin1)
                dma_emit("q", lambda E_: E_.dma_start(wfdump_d, wfin_sb[:, :, :]))
            if "ffn1" in stages:
                if "ssm" in stages:
                    ln2 = LNPipe(save_stats=True, zt0=ZT2_F, zt1=ZT2_F + 1024,
                                 psA="pB", psB="yB")
                ffn1_vm = emit_ffn(z1, wfin1, nexp=nexp, ln_tail=ln2)
                ffn1_tm = cnt["t"]
        ffn1_slices = slice_idx[0]

        ssm_vm = ssm_tm = ssm_sm = 0
        if "ssm" in stages:
            if ln2 is None:
                ln2 = LNPipe(save_stats=True, zt0=ZT2_F, zt1=ZT2_F + 1024,
                             psA="pB", psB="yB")
            z2 = ln2.finish()
            ln2_sm, ln2_vm = cnt["s"], cnt["v"]
            if "ffn3" in stages:
                ln3 = LNPipe(save_stats=False, zt0=ZT_F, zt1=ZT_F + 1024,
                             psA="pA", psB="pB")
            ssm_vm, ssm_tm, ssm_sm = emit_ssm(
                z2, ring_free_dep=ffn1_slices,
                w_deps=[("s", ln2_sm), ("v", ln2_vm)], ln_tail=ln3,
                xvm=cnt["v"])

        if "ffn3" in stages:
            if ln3 is None:
                ln3 = LNPipe(save_stats=False, zt0=ZT_F, zt1=ZT_F + 1024,
                             psA="pA", psB="pB")
            z3 = ln3.finish(xdep=("v", ssm_vm))
            ltm3 = emit_gates(z3)
            wfin3 = emit_gate_math(ltm3)
            wait("g", "t", ssm_tm)
            wait("g", "v", ssm_vm)
            wait("g", "s", ssm_sm)
            ffn3_vm = emit_ffn(z3, wfin3, nexp=nexp)

        wait("q", "v", cnt["v"])
        wait("q", "s", cnt["s"])
        wait("q", "t", cnt["t"])
        for c in range(4):
            dma_emit("q", lambda E_, c=c: E_.dma_start(
                out_d[:, 4 * c:4 * c + 4, :], x_sb[:, 4 * c:4 * c + 4, :]))
        ops["q"].append(lambda E_: E_.wait_ge(SEM["q"], dma["q"]))

        blk = E(nc.Block())

        @blk.sync
        def _(eng):
            for fn in ops["q"]:
                fn(eng)

        @blk.gpsimd
        def _(eng):
            for fn in ops["g"]:
                fn(eng)

        @blk.tensor
        def _(eng):
            for fn in ops["t"]:
                fn(eng)

        @blk.scalar
        def _(eng):
            for fn in ops["s"]:
                fn(eng)

        @blk.vector
        def _(eng):
            for fn in ops["v"]:
                fn(eng)

    return nc


_NC_CACHE = [None]


def kernel(**inputs):
    xs, shared = _host_layout(inputs)
    if _NC_CACHE[0] is None:
        _NC_CACHE[0] = build_nc()
    nc = _NC_CACHE[0]
    in_maps = []
    for b in range(8):
        m = {"x": xs[b]}
        m.update(shared)
        in_maps.append(m)
    res = run_bass_kernel_spmd(nc, in_maps, core_ids=list(range(8)))
    outs = []
    for b in range(8):
        o = np.asarray(res.results[b]["out"])
        outs.append(np.ascontiguousarray(o.transpose(1, 0, 2)).reshape(2048, 1024))
    return np.stack(outs).astype(np.float32)


# version 17
# speedup vs baseline: 1.1941x; 1.0014x over previous
import numpy as np
import ml_dtypes
from contextlib import ExitStack

import concourse.bass as bass
import concourse.mybir as mybir
from concourse.bass_utils import run_bass_kernel_spmd

F32 = mybir.dt.float32
BF16 = mybir.dt.bfloat16
AF = mybir.ActivationFunctionType
ALU = mybir.AluOpType
AX = mybir.AxisListType

D = 1024
T = 2048
I = 1408
NE = 10
NI = 11
NTC = 16
RING = 27
EPS = 1e-5

SCR_BF = 55296
ACT_OFF = 27648
SG_F = 25088
FIL_F = (ACT_OFF + 8 * 2048) // 2
ZT2_F = 3072
WRT_BF = 0
WIT_BF = 8192
R_F = 8192
I_F = 12288
Z2_F = 16384
LA_F = 20480
A_F = 22528
ZT_F = 24576


def _host_layout(inputs):
    g = {k: np.ascontiguousarray(np.asarray(v, dtype=np.float32))
         for k, v in inputs.items()}
    x = g["L0_attn_x"]

    def lay_up(w):
        return w.reshape(8, 128, 11, 128).transpose(2, 1, 0, 3)

    wg_l, wu_l, wd_l = [], [], []
    for e in range(8):
        wg_l.append(lay_up(g["L0_ffn_wg"][e]))
        wu_l.append(lay_up(g["L0_ffn_wu"][e]))
        wd_l.append(g["L0_ffn_wd"][e].reshape(11, 128, 1024))
    for h in range(2):
        wg_l.append(lay_up(g["L0_ffn_swg"][:, h * I:(h + 1) * I]))
        wu_l.append(lay_up(g["L0_ffn_swu"][:, h * I:(h + 1) * I]))
        wd_l.append(g["L0_ffn_swd"][h * I:(h + 1) * I, :].reshape(11, 128, 1024))
    wgs = np.ascontiguousarray(np.stack(wg_l))
    wus = np.ascontiguousarray(np.stack(wu_l))
    wds = np.ascontiguousarray(np.stack(wd_l))

    def lay_sq(w):
        return np.ascontiguousarray(w.reshape(8, 128, 1024).transpose(1, 0, 2))

    wrt = lay_sq(np.ascontiguousarray(g["L0_attn_W_r"].T))
    wit = lay_sq(np.ascontiguousarray(g["L0_attn_W_i"].T))
    gwt = np.ascontiguousarray(
        g["L0_ffn_gate_w"].T.reshape(8, 128, 8).transpose(1, 0, 2))
    lam = np.ascontiguousarray(g["L0_attn_lambd"].reshape(8, 128).T)
    idb = np.eye(128, dtype=ml_dtypes.bfloat16)
    idf = np.eye(128, dtype=np.float32)
    cst = np.tile(np.array([[EPS, 0.0, 0.0, 0.0]], dtype=np.float32), (128, 1))

    xs = [np.ascontiguousarray(x[b].reshape(16, 128, 1024).transpose(1, 0, 2))
          for b in range(8)]
    shared = dict(wgs=wgs, wus=wus, wds=wds, wrt=wrt, wit=wit, gwt=gwt,
                  lam=lam, idb=idb, idf=idf, cst=cst)
    return xs, shared


def build_nc(stages=("ffn1", "ssm", "ffn3")):
    nc = bass.Bass("TRN2", target_bir_lowering=False)

    x_d = nc.dram_tensor("x", [128, 16, 1024], F32, kind="ExternalInput").ap()
    wgs_d = nc.dram_tensor("wgs", [NE, NI, 128, 8, 128], F32, kind="ExternalInput").ap()
    wus_d = nc.dram_tensor("wus", [NE, NI, 128, 8, 128], F32, kind="ExternalInput").ap()
    wds_d = nc.dram_tensor("wds", [NE, NI, 128, 1024], F32, kind="ExternalInput").ap()
    wrt_d = nc.dram_tensor("wrt", [128, 8, 1024], F32, kind="ExternalInput").ap()
    wit_d = nc.dram_tensor("wit", [128, 8, 1024], F32, kind="ExternalInput").ap()
    gwt_d = nc.dram_tensor("gwt", [128, 8, 8], F32, kind="ExternalInput").ap()
    lam_d = nc.dram_tensor("lam", [128, 8], F32, kind="ExternalInput").ap()
    idb_d = nc.dram_tensor("idb", [128, 128], BF16, kind="ExternalInput").ap()
    idf_d = nc.dram_tensor("idf", [128, 128], F32, kind="ExternalInput").ap()
    cst_d = nc.dram_tensor("cst", [128, 4], F32, kind="ExternalInput").ap()
    out_d = nc.dram_tensor("out", [128, 16, 1024], F32, kind="ExternalOutput").ap()
    dbg = any(str(s).startswith("dump") or str(s).startswith("nexp") for s in stages)
    if dbg:
        zdump_d = nc.dram_tensor("zdump", [128, 8, 2048], BF16, kind="ExternalOutput").ap()
        wfdump_d = nc.dram_tensor("wfdump", [128, 16, 8], F32, kind="ExternalOutput").ap()
    if "dump_ssm" in stages:
        sdump_d = nc.dram_tensor("sdump", [128, 5, 8, 512], F32, kind="ExternalOutput").ap()
        ldump_d = nc.dram_tensor("ldump", [128, 16, 2], F32, kind="ExternalOutput").ap()

    with ExitStack() as stack:
        E = stack.enter_context
        x_sb = E(nc.sbuf_tensor("x_sb", [128, 16, 1024], F32))
        z_sb = E(nc.sbuf_tensor("z_sb", [128, 8, 2048], BF16))
        scr = E(nc.sbuf_tensor("scr", [128, SCR_BF], BF16))
        gwt_sb = E(nc.sbuf_tensor("gwt_sb", [128, 8, 8], BF16))
        lam_sb = E(nc.sbuf_tensor("lam_sb", [128, 8], F32))
        l8_sb = E(nc.sbuf_tensor("l8_sb", [128, 8], F32))
        cst_sb = E(nc.sbuf_tensor("cst_sb", [128, 4], F32))
        idb_sb = E(nc.sbuf_tensor("idb_sb", [128, 128], BF16))
        idf_sb = E(nc.sbuf_tensor("idf_sb", [128, 128], F32))
        wfin_sb = E(nc.sbuf_tensor("wfin_sb", [128, 16, 8], F32))
        wfin_fl = wfin_sb.reshape([128, 128])
        ltm_sb = E(nc.sbuf_tensor("ltm_sb", [128, 16, 8], F32))
        lgt_sb = E(nc.sbuf_tensor("lgt_sb", [128, 16, 8], F32))
        stat_sb = E(nc.sbuf_tensor("stat_sb", [128, 48], F32))
        ln2_sb = E(nc.sbuf_tensor("ln2_sb", [128, 16, 2], F32))
        ln2_fl = ln2_sb.reshape([128, 32])
        carry_sb = E(nc.sbuf_tensor("carry_sb", [128, 8], F32))
        gmax_sb = E(nc.sbuf_tensor("gmax_sb", [128, 16], F32))
        gm2_sb = E(nc.sbuf_tensor("gm2_sb", [128, 16], F32))
        gden_sb = E(nc.sbuf_tensor("gden_sb", [128, 16], F32))

        PT = {
            "pA": E(nc.psum_tensor("pA", [128, 1024], F32)),
            "pB": E(nc.psum_tensor("pB", [128, 1024], F32)),
            "yA": E(nc.psum_tensor("yA", [128, 1024], F32)),
            "yB": E(nc.psum_tensor("yB", [128, 1024], F32)),
        }

        SEM = {
            "t": E(nc.semaphore("sem_t")),
            "v": E(nc.semaphore("sem_v")),
            "s": E(nc.semaphore("sem_s")),
            "g": E(nc.semaphore("sem_g")),
            "q": E(nc.semaphore("sem_q")),
            "f": E(nc.semaphore("sem_f")),
            "G": E(nc.semaphore("sem_G")),
            "Q": E(nc.semaphore("sem_Q")),
        }

        scrf = scr.bitcast(F32)

        ops = {k: [] for k in "tvsgq"}
        cnt = {"t": 0, "v": 0, "s": 0, "f": 0, "G": 0, "Q": 0}
        dma = {"g": 0, "q": 0}
        known = {k: {} for k in "tvsgq"}

        def wait(eng, key, val):
            if val is None or val <= 0:
                return
            if val <= known[eng].get(key, 0):
                return
            known[eng][key] = val
            ops[eng].append(lambda E_, s=SEM[key], v=val: E_.wait_ge(s, v))

        def emit(eng, fn, incs=()):
            norm = [(k, -1) if isinstance(k, str) else k for k in incs]
            attach = [(k, i) for k, i in norm if i is not None]
            post = [k for k, i in norm if i is None]
            idxs = [i for _, i in attach]
            assert len(set(idxs)) == len(idxs)
            own = cnt[eng] + sum(1 for k, i in attach if k == eng)
            def run(E_, fn=fn, attach=tuple(attach), post=tuple(post), own=own):
                insts = fn(E_)
                for k, idx in attach:
                    insts[idx].then_inc(SEM[k], 1)
                for k in post:
                    E_.wait_ge(SEM[eng], own).then_inc(SEM[k], 1)
            ops[eng].append(run)
            for k, _ in norm:
                cnt[k] += 1

        def dma_emit(eng, fn):
            def run(E_, fn=fn):
                fn(E_).then_inc(SEM[eng], 16)
            ops[eng].append(run)
            dma[eng] += 16
            return dma[eng]

        def dma_flush(eng):
            pub = "G" if eng == "g" else "Q"
            v = dma[eng]
            def run(E_, v=v):
                E_.wait_ge(SEM[eng], v)
                E_.nop().then_inc(SEM[pub], 1)
            ops[eng].append(run)
            cnt[pub] += 1
            return cnt[pub]

        pg = {k: None for k in PT}
        ztg_map = {}
        sqm_map = {}
        sgg = [None, None]

        def map_wait(eng, mp, key):
            g = mp.get(key)
            if g is not None:
                wait(eng, g[0], g[1])

        def res_wait(eng, guards, idxs):
            for n in idxs:
                g = guards[n]
                if g is not None:
                    wait(eng, g[0], g[1])

        def psum_wait(eng, names):
            for n in names:
                if pg[n] is not None:
                    wait(eng, pg[n][0], pg[n][1])

        def psum_set(names, key, val):
            for n in names:
                pg[n] = (key, val)

        def ring_slot(k):
            s0 = (k % RING) * 1024
            return scr[:, s0:s0 + 1024]

        def act_ap(i):
            o = ACT_OFF + i * 2048
            return scr[:, o:o + 2048]

        def sg_ap(p):
            o = SG_F + p * 1024
            return scrf[:, o:o + 1024]

        def zt_ap(p):
            o = ZT_F + p * 1024
            return scrf[:, o:o + 1024]

        def zt_bf(p):
            o = 2 * (ZT_F + p * 1024)
            return scr[:, o:o + 1024]

        def fil_ap():
            return scrf[:, FIL_F:FIL_F + 512]

        x_ready = []
        for c in range(4):
            dma_emit("q", lambda E_, c=c: E_.dma_start(
                x_sb[:, 4 * c:4 * c + 4, :], x_d[:, 4 * c:4 * c + 4, :]))
            x_ready.append(dma_flush("q"))
        dma_emit("q", lambda E_: E_.dma_start(cst_sb[:, :], cst_d))
        dma_emit("q", lambda E_: E_.dma_start(lam_sb[:, :], lam_d))
        dma_emit("q", lambda E_: E_.dma_start(idb_sb[:, :], idb_d))
        dma_emit("q", lambda E_: E_.dma_start(idf_sb[:, :], idf_d))
        cst_ready = lam_ready = idb_ready = idf_ready = dma_flush("q")
        dma_emit("g", lambda E_: E_.dma_start(gwt_sb[:, :, :], gwt_d))
        gwt_ready = dma_flush("g")

        wait("s", "Q", lam_ready)
        emit("s", lambda E_: [E_.activation(l8_sb[:, :], lam_sb[:, :], AF.Sigmoid)],
             incs=["s"])
        wait("s", "s", cnt["s"])
        emit("s", lambda E_: [E_.activation(l8_sb[:, :], l8_sb[:, :], AF.Ln)],
             incs=["s"])
        l8_sm = cnt["s"]
        wait("v", "s", l8_sm)
        emit("v", lambda E_: [E_.tensor_scalar(l8_sb[:, :], l8_sb[:, :], 8.0,
                                               None, ALU.mult)], incs=["v"])
        l8_vm = cnt["v"]

        class LNPipe:
            def __init__(self, save_stats, zt0, zt1, psA, psB):
                self.save = save_stats
                self.zt = (zt0, zt1)
                self.ps = (psA, psB)
                self.k = 0
                self.z_evac = []
                self.sq_sm = [0] * 6
                self.red_vm = [0] * 6
                self.m_sm = [0] * 6
                self.nvs_vm = [0] * 6
                self.sd_sm = [0] * 6
                self.rstd_vm = [0] * 6

            def zt_f(self, p):
                o = self.zt[p]
                return scrf[:, o:o + 1024]

            def zt_b(self, p):
                o = 2 * self.zt[p]
                return scr[:, o:o + 1024]

            def step(self, xdep=None):
                k = self.k
                self.k += 1
                if k < NTC:
                    tc = k
                    par = (k % 6) * 8
                    sl = k % 2
                    off = self.zt[sl]
                    xc = x_sb[:, tc, :]
                    wait("s", "Q", x_ready[tc // 4])
                    wait("s", "Q", cst_ready)
                    if xdep is not None:
                        wait("s", xdep[0], xdep[1])
                    map_wait("s", ztg_map, off)
                    dsc = self.zt_f(sl)
                    emit("s", lambda E_, xc=xc, dsc=dsc, par=par: [
                        E_.activation(dsc, xc, AF.Square,
                                      accum_out=stat_sb[:, par + 1:par + 2])],
                        incs=["s"])
                    sqm_map[off] = ("s", cnt["s"])
                    self.sq_sm[k % 6] = cnt["s"]
                    wait("v", "Q", x_ready[tc // 4])
                    if xdep is not None:
                        wait("v", xdep[0], xdep[1])
                    emit("v", lambda E_, xc=xc, par=par: [
                        E_.reduce_sum(stat_sb[:, par + 0:par + 1], xc, AX.X)],
                        incs=["v"])
                    self.red_vm[k % 6] = cnt["v"]
                if 1 <= k <= NTC:
                    p1 = ((k - 1) % 6) * 8
                    wait("s", "v", self.red_vm[(k - 1) % 6])
                    wait("s", "s", self.sq_sm[(k - 1) % 6])
                    emit("s", lambda E_, p1=p1: [
                        E_.mul(stat_sb[:, p1 + 2:p1 + 3],
                               stat_sb[:, p1 + 0:p1 + 1], 1.0 / 1024.0)])
                    emit("s", lambda E_, p1=p1: [
                        E_.mul(stat_sb[:, p1 + 3:p1 + 4],
                               stat_sb[:, p1 + 1:p1 + 2], 1.0 / 1024.0)],
                        incs=["s"])
                    self.m_sm[(k - 1) % 6] = cnt["s"]
                if 2 <= k <= NTC + 1:
                    p2 = ((k - 2) % 6) * 8
                    wait("v", "s", self.m_sm[(k - 2) % 6])
                    emit("v", lambda E_, p2=p2: [
                        E_.scalar_tensor_tensor(
                            stat_sb[:, p2 + 4:p2 + 5],
                            stat_sb[:, p2 + 2:p2 + 3],
                            stat_sb[:, p2 + 2:p2 + 3],
                            stat_sb[:, p2 + 3:p2 + 4],
                            ALU.mult, ALU.subtract)], incs=["v"])
                    self.nvs_vm[(k - 2) % 6] = cnt["v"]
                if 3 <= k <= NTC + 2:
                    p3 = ((k - 3) % 6) * 8
                    wait("s", "v", self.nvs_vm[(k - 3) % 6])
                    emit("s", lambda E_, p3=p3: [
                        E_.activation(stat_sb[:, p3 + 5:p3 + 6],
                                      stat_sb[:, p3 + 4:p3 + 5], AF.Sqrt,
                                      scale=-1.0, bias=cst_sb[:, 0:1])],
                        incs=["s"])
                    self.sd_sm[(k - 3) % 6] = cnt["s"]
                if 4 <= k <= NTC + 3:
                    p4 = ((k - 4) % 6) * 8
                    wait("v", "s", self.sd_sm[(k - 4) % 6])
                    emit("v", lambda E_, p4=p4: [
                        E_.reciprocal(stat_sb[:, p4 + 6:p4 + 7],
                                      stat_sb[:, p4 + 5:p4 + 6])], incs=["v"])
                    self.rstd_vm[(k - 4) % 6] = cnt["v"]
                if 5 <= k:
                    tc = k - 5
                    p5 = (tc % 6) * 8
                    sl2 = tc % 2
                    off2 = self.zt[sl2]
                    wait("v", "v", self.rstd_vm[tc % 6])
                    if self.save:
                        emit("v", lambda E_, p5=p5, tc=tc: [
                            E_.tensor_copy(ln2_fl[:, tc * 2:tc * 2 + 1],
                                           stat_sb[:, p5 + 2:p5 + 3])])
                        emit("v", lambda E_, p5=p5, tc=tc: [
                            E_.tensor_copy(ln2_fl[:, tc * 2 + 1:tc * 2 + 2],
                                           stat_sb[:, p5 + 6:p5 + 7])])
                    ztm = self.zt_b(sl2)
                    map_wait("v", sqm_map, off2)
                    map_wait("v", ztg_map, off2)
                    emit("v", lambda E_, tc=tc, p5=p5, ztm=ztm: [
                        E_.tensor_scalar(ztm, x_sb[:, tc, :],
                                         stat_sb[:, p5 + 2:p5 + 3],
                                         stat_sb[:, p5 + 6:p5 + 7],
                                         ALU.subtract, ALU.mult)], incs=["v"])
                    ztm_vm = cnt["v"]
                    pname = self.ps[sl2]
                    psb = PT[pname].bitcast(BF16)
                    wait("t", "v", ztm_vm)
                    wait("t", "Q", idb_ready)
                    psum_wait("t", [pname])
                    def tr8(E_, ztm=ztm, psb=psb):
                        insts = []
                        for dc in range(8):
                            insts.append(E_.transpose(
                                psb[:, dc * 128:dc * 128 + 128],
                                ztm[:, dc * 128:dc * 128 + 128], idb_sb[:, :]))
                        return insts
                    emit("t", tr8, incs=["t"])
                    ztg_map[off2] = ("t", cnt["t"])
                    wait("s", "t", cnt["t"])
                    emit("s", lambda E_, psb=psb, tc=tc: [
                        E_.copy(
                            z_sb[:, 0:8, tc * 128:tc * 128 + 128],
                            psb[:, 0:1024].rearrange("p (a b) -> p a b", b=128))],
                        incs=["s"])
                    psum_set([pname], "s", cnt["s"])
                    self.z_evac.append(cnt["s"])

            def finish(self, xdep=None):
                while self.k < NTC + 5:
                    self.step(xdep=xdep)
                return self.z_evac

        def emit_gates(z_evac):
            wait("t", "G", gwt_ready)
            psum_wait("t", ["pA"])
            for tc in range(NTC):
                wait("t", "s", z_evac[tc])
                def gmm(E_, tc=tc):
                    insts = []
                    for kc in range(8):
                        E_.ldweights(z_sb[:, kc, tc * 128:tc * 128 + 128])
                        insts.append(E_.matmul(
                            PT["pA"][:, tc * 8:tc * 8 + 8],
                            z_sb[:, kc, tc * 128:tc * 128 + 128],
                            gwt_sb[:, kc, :],
                            start=(kc == 0), stop=(kc == 7)))
                    return insts
                emit("t", gmm, incs=["t"])
            wait("v", "t", cnt["t"])
            emit("v", lambda E_: [
                E_.tensor_copy(ltm_sb[:, :, :],
                               PT["pA"][:, 0:128].rearrange(
                                   "p (a b) -> p a b", b=8))], incs=["v"])
            psum_set(["pA"], "v", cnt["v"])
            return cnt["v"]

        def emit_gate_math(ltm_vm):
            l3 = ltm_sb[:, :, :]
            p3 = lgt_sb[:, :, :]
            t3 = wfin_sb[:, :, :]
            m1b = gmax_sb[:, :].unsqueeze(2).to_broadcast([128, 16, 8])
            m2b = gm2_sb[:, :].unsqueeze(2).to_broadcast([128, 16, 8])
            rdb = gden_sb[:, :].unsqueeze(2).to_broadcast([128, 16, 8])
            wait("v", "v", ltm_vm)
            emit("v", lambda E_: [E_.tensor_reduce(gmax_sb[:, :], l3, AX.X, ALU.max)],
                 incs=["v"])
            wait("v", "v", cnt["v"])
            emit("v", lambda E_: [E_.tensor_tensor(p3, l3, m1b, ALU.subtract)],
                 incs=["v"])
            wait("s", "v", cnt["v"])
            emit("s", lambda E_: [E_.activation(p3, p3, AF.Exp)], incs=["s"])
            emit("v", lambda E_: [E_.tensor_tensor(t3, l3, m1b, ALU.is_ge)],
                 incs=["v"])
            wait("v", "v", cnt["v"])
            emit("v", lambda E_: [E_.scalar_tensor_tensor(t3, t3, -1e30, l3,
                                                          ALU.mult, ALU.add)],
                 incs=["v"])
            wait("v", "v", cnt["v"])
            emit("v", lambda E_: [E_.tensor_reduce(gm2_sb[:, :], t3, AX.X, ALU.max)],
                 incs=["v"])
            wait("v", "v", cnt["v"])
            emit("v", lambda E_: [E_.tensor_tensor(t3, l3, m2b, ALU.is_ge)],
                 incs=["v"])
            wait("v", "v", cnt["v"])
            wait("v", "s", cnt["s"])
            emit("v", lambda E_: [E_.tensor_tensor(p3, p3, t3, ALU.mult)],
                 incs=["v"])
            wait("v", "v", cnt["v"])
            emit("v", lambda E_: [E_.tensor_reduce(gden_sb[:, :], p3, AX.X, ALU.add)],
                 incs=["v"])
            wait("v", "v", cnt["v"])
            emit("v", lambda E_: [E_.reciprocal(gden_sb[:, :], gden_sb[:, :])],
                 incs=["v"])
            wait("v", "v", cnt["v"])
            emit("v", lambda E_: [E_.tensor_tensor(p3, p3, rdb, ALU.mult)],
                 incs=["v"])
            wait("v", "v", cnt["v"])
            emit("v", lambda E_: [E_.tensor_scalar(wfin_sb[:, :, :], p3, 0.5,
                                                   None, ALU.mult)], incs=["v"])
            return cnt["v"]

        slice_idx = [0]
        unit_hist = []

        def ring_dma(src_ap, flush=True):
            k = slice_idx[0]
            wait("g", "f", k - RING + 1)
            dst = ring_slot(k)
            dma_emit("g", lambda E_, dst=dst, src=src_ap: E_.dma_start(dst, src))
            slice_idx[0] += 1
            v = dma_flush("g") if flush else None
            return k, v

        ymark = [0] * NTC

        def emit_ffn(z_evac, wfin_vm, nexp=NE, e0=0, ln_tail=None):
            z_all = z_evac[-1]
            for e in range(e0, nexp):
                wg_slots, wu_slots, wd_slots = [], [], []
                wg_done, wu_done, wd_done = [], [], []
                for i in range(NI):
                    k, _ = ring_dma(wgs_d[e, i], flush=False)
                    wg_slots.append(k)
                    k, v = ring_dma(wus_d[e, i], flush=True)
                    wu_slots.append(k); wu_done.append(v)
                for i in range(NI):
                    k, v = ring_dma(wds_d[e, i], flush=(i == NI - 1))
                    wd_slots.append(k)
                    if v is not None:
                        wd_done.append(v)

                for i in range(NI):
                    for tp in range(2):
                        par = (i * 2 + tp) % 2
                        hg_n = "pA" if par == 0 else "yA"
                        hu_n = "pB" if par == 0 else "yB"
                        hg, hu = PT[hg_n], PT[hu_n]
                        wait("t", "G", wu_done[i])
                        wait("t", "s", z_all)
                        psum_wait("t", [hg_n, hu_n])
                        wg_ap = ring_slot(wg_slots[i]).rearrange(
                            "p (a b) -> p a b", b=128)
                        wu_ap = ring_slot(wu_slots[i]).rearrange(
                            "p (a b) -> p a b", b=128)
                        t0 = tp * 1024
                        incs = ([("t", 15), ("t", -1)] if tp == 0
                                else [("t", -1), ("f", 15), ("f", None)])
                        def up(E_, wg_ap=wg_ap, wu_ap=wu_ap, hg=hg, hu=hu, t0=t0):
                            insts = []
                            for d in range(8):
                                E_.ldweights(wg_ap[:, d, :])
                                insts.append(E_.matmul(
                                    hg[:, 0:512], wg_ap[:, d, :],
                                    z_sb[:, d, t0:t0 + 512],
                                    start=(d == 0), stop=(d == 7)))
                                insts.append(E_.matmul(
                                    hg[:, 512:1024], wg_ap[:, d, :],
                                    z_sb[:, d, t0 + 512:t0 + 1024],
                                    start=(d == 0), stop=(d == 7)))
                            for d in range(8):
                                E_.ldweights(wu_ap[:, d, :])
                                insts.append(E_.matmul(
                                    hu[:, 0:512], wu_ap[:, d, :],
                                    z_sb[:, d, t0:t0 + 512],
                                    start=(d == 0), stop=(d == 7)))
                                insts.append(E_.matmul(
                                    hu[:, 512:1024], wu_ap[:, d, :],
                                    z_sb[:, d, t0 + 512:t0 + 1024],
                                    start=(d == 0), stop=(d == 7)))
                            return insts
                        emit("t", up, incs=incs)
                        tmark = cnt["t"]
                        hg_mark = tmark - 1 if tp == 0 else tmark
                        wait("s", "t", hg_mark)
                        res_wait("s", sgg, [par])
                        emit("s", lambda E_, hg=hg, par=par: [
                            E_.activation(sg_ap(par), hg[:, :], AF.Silu)],
                            incs=["s"])
                        wait("v", "s", cnt["s"])
                        wait("v", "t", tmark)
                        dst = act_ap(i)[:, tp * 1024:tp * 1024 + 1024]
                        emit("v", lambda E_, hu=hu, par=par, dst=dst: [
                            E_.tensor_tensor(dst, hu[:, :], sg_ap(par), ALU.mult)],
                            incs=["v"])
                        vm = cnt["v"]
                        psum_set([hg_n, hu_n], "v", vm)
                        sgg[par] = ("v", vm)
                acts_vm = cnt["v"]

                for tc in range(NTC):
                    y_n = "pA" if tc % 2 == 0 else "yA"
                    yps = PT[y_n]
                    wait("t", "G", wd_done[-1])
                    wait("t", "v", acts_vm)
                    psum_wait("t", [y_n])
                    last_group = (tc == NTC - 1)
                    if last_group:
                        incs = ([("t", -1)]
                                + [("f", 2 * i + 2) for i in range(NI - 1)]
                                + [("f", None)])
                    else:
                        incs = ["t"]
                    def down(E_, tc=tc, yps=yps, wd_slots=tuple(wd_slots)):
                        insts = []
                        for i in range(NI):
                            a_ap = act_ap(i)[:, tc * 128:tc * 128 + 128]
                            w_ap = ring_slot(wd_slots[i])
                            E_.ldweights(a_ap)
                            insts.append(E_.matmul(
                                yps[:, 0:512], a_ap, w_ap[:, 0:512],
                                start=(i == 0), stop=(i == NI - 1)))
                            insts.append(E_.matmul(
                                yps[:, 512:1024], a_ap, w_ap[:, 512:1024],
                                start=(i == 0), stop=(i == NI - 1)))
                        return insts
                    emit("t", down, incs=incs)
                    wait("v", "t", cnt["t"])
                    wait("v", "v", ymark[tc])
                    if e < 8:
                        wait("v", "v", wfin_vm)
                        sc_ap = wfin_fl[:, tc * 8 + e:tc * 8 + e + 1]
                        emit("v", lambda E_, yps=yps, tc=tc, sc_ap=sc_ap: [
                            E_.scalar_tensor_tensor(
                                x_sb[:, tc, :], yps[:, :], sc_ap,
                                x_sb[:, tc, :], ALU.mult, ALU.add)], incs=["v"])
                    else:
                        emit("v", lambda E_, yps=yps, tc=tc: [
                            E_.scalar_tensor_tensor(
                                x_sb[:, tc, :], yps[:, :], 0.5,
                                x_sb[:, tc, :], ALU.mult, ALU.add)], incs=["v"])
                    ymark[tc] = cnt["v"]
                    psum_set([y_n], "v", cnt["v"])
                    if ln_tail is not None and e == nexp - 1:
                        ln_tail.step(xdep=("v", cnt["v"]))
            return cnt["v"]

        def emit_ssm(z_evac, ring_free_dep, w_deps=(), ln_tail=None, xvm=0):
            wait("g", "f", ring_free_dep)
            for key_, val_ in w_deps:
                wait("g", key_, val_)
            wrt_sb = scr[:, WRT_BF:WRT_BF + 8192].rearrange("p (a b) -> p a b", b=1024)
            wit_sb = scr[:, WIT_BF:WIT_BF + 8192].rearrange("p (a b) -> p a b", b=1024)
            dma_emit("g", lambda E_: E_.dma_start(wrt_sb, wrt_d))
            dma_emit("g", lambda E_: E_.dma_start(wit_sb, wit_d))
            wit_done = dma_flush("g")

            r_ap = scrf[:, R_F:R_F + 4096].rearrange("p (a b) -> p a b", b=512)
            i_ap = scrf[:, I_F:I_F + 4096].rearrange("p (a b) -> p a b", b=512)
            z2_ap = scrf[:, Z2_F:Z2_F + 4096].rearrange("p (a b) -> p a b", b=512)
            a8_ap = scrf[:, LA_F:LA_F + 4096].rearrange("p (a b) -> p a b", b=512)

            RIP = [("yA", 0), ("yA", 512), ("yB", 0), ("yB", 512)]
            rip_guard = [pg["yA"], pg["yA"], pg["yB"], pg["yB"]]

            z_all = z_evac[-1]
            bi_vm = [0, 0]
            chain_sm = [0, 0]
            scan_vm = [0, 0]
            bt_vm = [0, 0]

            def emit_A1(tt):
                for q in range(4):
                    tc = tt * 4 + q
                    par = q % 2
                    zt = zt_ap(par)
                    mcol = ln2_fl[:, tc * 2:tc * 2 + 1]
                    rstd = ln2_fl[:, tc * 2 + 1:tc * 2 + 2]
                    zoff = ZT_F + par * 1024
                    map_wait("v", ztg_map, zoff)
                    map_wait("v", sqm_map, zoff)
                    wait("v", "v", xvm)
                    emit("v", lambda E_, tc=tc, zt=zt, mcol=mcol, rstd=rstd: [
                        E_.tensor_scalar(zt, x_sb[:, tc, :], mcol, rstd,
                                         ALU.subtract, ALU.mult)], incs=["v"])
                    pname = "pA" if par == 0 else "pB"
                    ps = PT[pname]
                    wait("t", "v", cnt["v"])
                    wait("t", "Q", idf_ready)
                    psum_wait("t", [pname])
                    def trz(E_, zt=zt, ps=ps):
                        insts = []
                        for dc in range(8):
                            insts.append(E_.transpose(
                                ps[:, dc * 128:dc * 128 + 128],
                                zt[:, dc * 128:dc * 128 + 128], idf_sb[:, :]))
                        return insts
                    emit("t", trz, incs=["t"])
                    ztg_map[zoff] = ("t", cnt["t"])
                    wait("s", "t", cnt["t"])
                    wait("s", "v", bi_vm[1])
                    emit("s", lambda E_, ps=ps, q=q: [
                        E_.copy(
                            z2_ap[:, 0:8, q * 128:q * 128 + 128],
                            ps[:, 0:1024].rearrange("p (a b) -> p a b", b=128))],
                        incs=["s"])
                    psum_set([pname], "s", cnt["s"])

            def emit_A2(tt, half, dribble=None):
                wait("t", "G", wit_done)
                wait("t", "s", z_all)
                dcs = list(range(half * 4, half * 4 + 4))
                gi = 0
                for rv in range(2):
                    w_sb = wrt_sb if rv == 0 else wit_sb
                    dst = r_ap if rv == 0 else i_ap
                    for j in dcs:
                        slot = (half * 8 + gi) % 4
                        pn, off = RIP[slot]
                        ps = PT[pn][:, off:off + 512]
                        if rip_guard[slot] is not None:
                            wait("t", rip_guard[slot][0], rip_guard[slot][1])
                        def rmm(E_, w_sb=w_sb, j=j, tt=tt, ps=ps):
                            insts = []
                            for k in range(8):
                                E_.ldweights(w_sb[:, k, j * 128:j * 128 + 128])
                                insts.append(E_.matmul(
                                    ps, w_sb[:, k, j * 128:j * 128 + 128],
                                    z_sb[:, k, tt * 512:tt * 512 + 512],
                                    start=(k == 0), stop=(k == 7)))
                            return insts
                        emit("t", rmm, incs=["t"])
                        wait("s", "t", cnt["t"])
                        wait("s", "v", scan_vm[half])
                        emit("s", lambda E_, ps=ps, dst=dst, j=j: [
                            E_.activation(dst[:, j, :], ps, AF.Sigmoid)],
                            incs=["s"])
                        rip_guard[slot] = ("s", cnt["s"])
                        gi += 1
                        if dribble is not None and gi % 2 == 0:
                            dribble[0].step(xdep=dribble[1])
                ri_sm = cnt["s"]
                wait("v", "s", ri_sm)
                wait("v", "v", l8_vm)
                for dc in dcs:
                    emit("v", lambda E_, dc=dc: [
                        E_.tensor_scalar(r_ap[:, dc, :], r_ap[:, dc, :],
                                         l8_sb[:, dc:dc + 1], None, ALU.mult)],
                        incs=["v"])
                la_vm = cnt["v"]
                for dc in dcs:
                    emit("v", lambda E_, dc=dc: [
                        E_.tensor_tensor(i_ap[:, dc, :], i_ap[:, dc, :],
                                         z2_ap[:, dc, :], ALU.mult)], incs=["v"])
                bi_vm[half] = cnt["v"]
                wait("s", "v", la_vm)
                wait("s", "v", bt_vm[half])
                for dc in dcs:
                    emit("s", lambda E_, dc=dc: [
                        E_.activation(a8_ap[:, dc, :], r_ap[:, dc, :], AF.Exp)],
                        incs=["s"])
                wait("s", "s", cnt["s"])
                for dc in dcs:
                    emit("s", lambda E_, dc=dc: [
                        E_.activation(r_ap[:, dc, :], r_ap[:, dc, :],
                                      AF.Exp, scale=2.0)], incs=["s"])
                wait("s", "s", cnt["s"])
                for dc in dcs:
                    emit("s", lambda E_, dc=dc: [
                        E_.activation(r_ap[:, dc, :], r_ap[:, dc, :],
                                      AF.Relu, scale=-1.0, bias=1.0)], incs=["s"])
                wait("s", "s", cnt["s"])
                for dc in dcs:
                    emit("s", lambda E_, dc=dc: [
                        E_.activation(r_ap[:, dc, :], r_ap[:, dc, :],
                                      AF.Sqrt)], incs=["s"])
                chain_sm[half] = cnt["s"]

            def emit_BV(tt, half):
                dcs = list(range(half * 4, half * 4 + 4))
                wait("v", "s", chain_sm[half])
                wait("v", "v", bi_vm[half])
                for dc in dcs:
                    emit("v", lambda E_, dc=dc: [
                        E_.tensor_tensor(i_ap[:, dc, :], i_ap[:, dc, :],
                                         r_ap[:, dc, :], ALU.mult)], incs=["v"])
                    wait("v", "v", cnt["v"])
                    init = 0.0 if tt == 0 else carry_sb[:, dc:dc + 1]
                    emit("v", lambda E_, dc=dc, init=init: [
                        E_.tensor_tensor_scan(a8_ap[:, dc, :], a8_ap[:, dc, :],
                                              i_ap[:, dc, :], init,
                                              ALU.mult, ALU.add)], incs=["v"])
                scan_vm[half] = cnt["v"]

            def emit_BT(tt, half):
                dcs = list(range(half * 4, half * 4 + 4))
                for dc in dcs:
                    pname = "pA" if dc % 2 == 0 else "pB"
                    ps = PT[pname]
                    wait("t", "v", scan_vm[half])
                    psum_wait("t", [pname])
                    def trh(E_, dc=dc, ps=ps):
                        insts = []
                        for qq in range(4):
                            insts.append(E_.transpose(
                                ps[:, qq * 128:qq * 128 + 128],
                                a8_ap[:, dc, qq * 128:qq * 128 + 128],
                                idf_sb[:, :]))
                        return insts
                    emit("t", trh, incs=["t"])
                    wait("v", "t", cnt["t"])
                    xdst = x_sb[:, tt * 4:tt * 4 + 4, dc * 128:dc * 128 + 128]
                    emit("v", lambda E_, ps=ps, xdst=xdst: [
                        E_.tensor_tensor(
                            xdst,
                            ps[:, 0:512].rearrange("p (a b) -> p a b", b=128),
                            xdst, ALU.add)])
                    wait("v", "v", scan_vm[half])
                    emit("v", lambda E_, dc=dc: [
                        E_.tensor_copy(carry_sb[:, dc:dc + 1],
                                       a8_ap[:, dc, 511:512])], incs=["v"])
                    psum_set([pname], "v", cnt["v"])
                bt_vm[half] = cnt["v"]

            for tt in range(5):
                if tt < 4:
                    emit_A1(tt)
                if tt >= 1:
                    emit_BV(tt - 1, 0)
                    emit_BT(tt - 1, 0)
                if tt < 4:
                    emit_A2(tt, 0)
                if tt >= 1:
                    emit_BV(tt - 1, 1)
                    emit_BT(tt - 1, 1)
                if tt < 4:
                    drb = None
                    if ln_tail is not None and tt >= 1:
                        drb = (ln_tail, ("v", bt_vm[1]))
                    emit_A2(tt, 1, dribble=drb)
            for slot in range(4):
                if rip_guard[slot] is not None:
                    pn = RIP[slot][0]
                    if pg[pn] is None or pg[pn][1] < rip_guard[slot][1]:
                        pg[pn] = rip_guard[slot]
            return cnt["v"], cnt["t"], cnt["s"]

        nexp = NE
        for s_ in stages:
            if str(s_).startswith("nexp:"):
                nexp = int(str(s_).split(":")[1])
        ffn1_tm = 0
        ln2 = None
        ln3 = None
        if "ffn1" in stages or dbg:
            ln1 = LNPipe(save_stats=False, zt0=ZT_F, zt1=ZT_F + 1024,
                         psA="pA", psB="pB")
            z1 = ln1.finish()
            if "dump_z" in stages:
                wait("q", "s", z1[-1])
                dma_emit("q", lambda E_: E_.dma_start(zdump_d, z_sb[:, :, :]))
            ltm1 = emit_gates(z1)
            wfin1 = emit_gate_math(ltm1)
            if dbg:
                wait("q", "v", wfin1)
                dma_emit("q", lambda E_: E_.dma_start(wfdump_d, wfin_sb[:, :, :]))
            if "ffn1" in stages:
                if "ssm" in stages:
                    ln2 = LNPipe(save_stats=True, zt0=ZT2_F, zt1=ZT2_F + 1024,
                                 psA="pB", psB="yB")
                ffn1_vm = emit_ffn(z1, wfin1, nexp=nexp, ln_tail=ln2)
                ffn1_tm = cnt["t"]
        ffn1_slices = slice_idx[0]

        ssm_vm = ssm_tm = ssm_sm = 0
        if "ssm" in stages:
            if ln2 is None:
                ln2 = LNPipe(save_stats=True, zt0=ZT2_F, zt1=ZT2_F + 1024,
                             psA="pB", psB="yB")
            z2 = ln2.finish()
            ln2_sm, ln2_vm = cnt["s"], cnt["v"]
            if "ffn3" in stages:
                ln3 = LNPipe(save_stats=False, zt0=ZT_F, zt1=ZT_F + 1024,
                             psA="pA", psB="pB")
            ssm_vm, ssm_tm, ssm_sm = emit_ssm(
                z2, ring_free_dep=ffn1_slices,
                w_deps=[("s", ln2_sm), ("v", ln2_vm)], ln_tail=ln3,
                xvm=cnt["v"])

        if "ffn3" in stages:
            if ln3 is None:
                ln3 = LNPipe(save_stats=False, zt0=ZT_F, zt1=ZT_F + 1024,
                             psA="pA", psB="pB")
            z3 = ln3.finish(xdep=("v", ssm_vm))
            ltm3 = emit_gates(z3)
            wfin3 = emit_gate_math(ltm3)
            wait("g", "t", ssm_tm)
            wait("g", "v", ssm_vm)
            wait("g", "s", ssm_sm)
            ffn3_vm = emit_ffn(z3, wfin3, nexp=nexp)

        wait("q", "v", cnt["v"])
        wait("q", "s", cnt["s"])
        wait("q", "t", cnt["t"])
        for c in range(4):
            dma_emit("q", lambda E_, c=c: E_.dma_start(
                out_d[:, 4 * c:4 * c + 4, :], x_sb[:, 4 * c:4 * c + 4, :]))
        ops["q"].append(lambda E_: E_.wait_ge(SEM["q"], dma["q"]))

        blk = E(nc.Block())

        @blk.sync
        def _(eng):
            for fn in ops["q"]:
                fn(eng)

        @blk.gpsimd
        def _(eng):
            for fn in ops["g"]:
                fn(eng)

        @blk.tensor
        def _(eng):
            for fn in ops["t"]:
                fn(eng)

        @blk.scalar
        def _(eng):
            for fn in ops["s"]:
                fn(eng)

        @blk.vector
        def _(eng):
            for fn in ops["v"]:
                fn(eng)

    return nc


_NC_CACHE = [None]


def kernel(**inputs):
    xs, shared = _host_layout(inputs)
    if _NC_CACHE[0] is None:
        _NC_CACHE[0] = build_nc()
    nc = _NC_CACHE[0]
    in_maps = []
    for b in range(8):
        m = {"x": xs[b]}
        m.update(shared)
        in_maps.append(m)
    res = run_bass_kernel_spmd(nc, in_maps, core_ids=list(range(8)))
    outs = []
    for b in range(8):
        o = np.asarray(res.results[b]["out"])
        outs.append(np.ascontiguousarray(o.transpose(1, 0, 2)).reshape(2048, 1024))
    return np.stack(outs).astype(np.float32)
